# version 29
# baseline (speedup 1.0000x reference)
import math
import os
import numpy as np
import concourse.bass as bass
import concourse.mybir as mybir
from concourse.bass_utils import run_bass_kernel_spmd

F32 = mybir.dt.float32
F16 = mybir.dt.float16
AX = mybir.AxisListType.X
OP = mybir.AluOpType
ACT = mybir.ActivationFunctionType

ENGINES = ("sync", "gpsimd", "tensor", "vector", "scalar")

D = 2048
T = 2048
B = 4
NH = 16
NKV = 2
DH = 128
E = 8
FD = 256
R = 16
OWN = 1024
HALF = 512
ALPHA = 1.0 / math.sqrt(DH)
EPS = 1e-5
EPS_HOST = 1e-5


class Ch:
    def __init__(self, handle, name):
        self.h = handle
        self.name = name
        self.n = 0

    def bump(self, amt):
        self.n += amt
        return self.n


class Mark:
    __slots__ = ("ch", "val")

    def __init__(self, ch=None, val=0):
        self.ch = ch
        self.val = val

    def waits(self):
        return [(self.ch, self.val)] if self.ch is not None else []


class Prog:
    def __init__(self, chans):
        self.q = {e: [] for e in ENGINES}
        self.chans = chans

    def emit(self, eng, fn, waits=(), inc=None, inc_amt=1, tag=""):
        ws = [(ch, val) for ch, val in waits if ch is not None]
        if inc is not None:
            inc.bump(inc_amt)
        self.q[eng].append((tuple(ws), fn, inc, inc_amt, tag))

    def audit(self):
        heads = {e: 0 for e in ENGINES}
        cnt = {id(c): 0 for c in self.chans}
        progress = True
        while progress:
            progress = False
            for e in ENGINES:
                while heads[e] < len(self.q[e]):
                    ws, _, inc, amt, tag = self.q[e][heads[e]]
                    if any(cnt[id(ch)] < val for ch, val in ws):
                        break
                    if inc is not None:
                        cnt[id(inc)] += amt
                    heads[e] += 1
                    progress = True
        stuck = {e: heads[e] for e in ENGINES if heads[e] < len(self.q[e])}
        if stuck:
            lines = []
            for e, h in stuck.items():
                ws, _, _, _, tag = self.q[e][h]
                missing = [(ch.name, val, cnt[id(ch)]) for ch, val in ws
                           if cnt[id(ch)] < val]
                lines.append(f"  {e}@{h}/{len(self.q[e])} tag={tag} missing={missing}")
            raise RuntimeError("schedule deadlock:\n" + "\n".join(lines))

    def lower(self, block):
        def mk(ename):
            def body(eng):
                for ws, fn, inc, amt, tag in self.q[ename]:
                    for ch, val in ws:
                        eng.wait_ge(ch.h, val)
                    inst = fn(eng)
                    if inc is not None:
                        assert inst is not None, f"op with inc returned None: {tag}"
                        inst.then_inc(inc.h, amt)
            return body
        block.sync(mk("sync"))
        block.gpsimd(mk("gpsimd"))
        block.tensor(mk("tensor"))
        block.vector(mk("vector"))
        block.scalar(mk("scalar"))


def build_nc():
    nc = bass.Bass("TRN2", target_bir_lowering=False,
                   detect_race_conditions=False)

    d_xseq = nc.dram_tensor("xseq", [T, D], F32, kind="ExternalInput").ap()
    d_xown = nc.dram_tensor("xown", [OWN, D], F32, kind="ExternalInput").ap()
    d_wq = nc.dram_tensor("wq", [D, D], F32, kind="ExternalInput").ap()
    d_wk = nc.dram_tensor("wk", [D, 256], F32, kind="ExternalInput").ap()
    d_wv = nc.dram_tensor("wv", [D, 256], F32, kind="ExternalInput").ap()
    d_wo = nc.dram_tensor("wo", [D, D], F32, kind="ExternalInput").ap()
    d_bqbo = nc.dram_tensor("bqbo", [2, D], F32, kind="ExternalInput").ap()
    d_bk = nc.dram_tensor("bk", [128, 2], F32, kind="ExternalInput").ap()
    d_bvrep = nc.dram_tensor("bvrep", [128, 256], F32, kind="ExternalInput").ap()
    d_G = nc.dram_tensor("gG", [D, E], F32, kind="ExternalInput").ap()
    d_gate = nc.dram_tensor("gate_w", [E, D, FD], F32, kind="ExternalInput").ap()
    d_up = nc.dram_tensor("up_w", [E, D, FD], F32, kind="ExternalInput").ap()
    d_down = nc.dram_tensor("down_w", [E, FD, D], F32, kind="ExternalInput").ap()
    d_lA = nc.dram_tensor("lora_A", [E, R, D], F32, kind="ExternalInput").ap()
    d_lB = nc.dram_tensor("lora_B", [E, D, R], F32, kind="ExternalInput").ap()
    d_ident = nc.dram_tensor("ident", [128, 128], F32, kind="ExternalInput").ap()
    d_mask = nc.dram_tensor("dmask", [128, 256], F32, kind="ExternalInput").ap()
    d_esel16 = nc.dram_tensor("esel16", [8, 128], F32, kind="ExternalInput").ap()
    d_ones = nc.dram_tensor("ones1", [33, 128], F32, kind="ExternalInput").ap()
    d_eps = nc.dram_tensor("epsv", [128, 1], F32, kind="ExternalInput").ap()
    d_out = nc.dram_tensor("out", [OWN, D], F32, kind="ExternalOutput").ap()
    DBG = bool(os.environ.get("KDBG"))
    if DBG:
        d_dbgO = [nc.dram_tensor(f"dbgO{i}", [128, 8192], F32,
                                 kind="ExternalOutput").ap() for i in range(2)]
        d_dbgP = nc.dram_tensor("dbgP", [128, 4096 + 2048], F32,
                                kind="ExternalOutput").ap()
        d_dbgQ = [nc.dram_tensor(f"dbgQ{i}", [128, 8192], F32,
                                 kind="ExternalOutput").ap() for i in range(2)]
        d_dbgK = nc.dram_tensor("dbgK", [128, 4096], F32, kind="ExternalOutput").ap()
        d_dbgV = nc.dram_tensor("dbgV", [128, 4096], F32, kind="ExternalOutput").ap()
        d_dbgXN = nc.dram_tensor("dbgXN", [128, 8192], F32, kind="ExternalOutput").ap()
        d_dbgLN = nc.dram_tensor("dbgLN", [128, 12288], F32, kind="ExternalOutput").ap()
        d_dbgST = nc.dram_tensor("dbgST", [128, 4096], F32, kind="ExternalOutput").ap()

    sb = nc.alloc_sbuf_tensor
    s_K = sb("s_K", [128, 2, T], F32)
    s_V = sb("s_V", [128, 16, 256], F32)
    s_XN = sb("s_XN", [128, 8192], F32)
    s_QO = sb("s_QO", [128, 8192], F32)
    s_ACC = sb("s_ACC", [128, 8192], F32)
    s_BIG = sb("s_BIG", [128, 14336], F32)
    s_SM = sb("s_SM", [128, 4096], F32)
    s_ident = sb("s_ident", [128, 128], F32)
    s_mask = sb("s_mask", [128, 256], F32)
    s_esel16 = sb("s_esel16", [8, 128], F32)
    s_bk = sb("s_bk", [128, 2], F32)
    s_bvrep = sb("s_bvrep", [128, 256], F32)
    s_G = sb("s_G", [128, 16, 8], F32)
    s_Gp = sb("s_Gp", [128, 128], F32)
    s_ones = sb("s_ones", [33, 128], F32)
    s_eps = sb("s_eps", [128, 1], F32)

    def v(arena, lo, sz, shape=None):
        ap = arena[:, lo:lo + sz]
        if shape is not None:
            ap = ap.rearrange("p (a b) -> p a b", a=shape[0], b=shape[1])
        return ap

    QO_Q = v(s_QO, 0, 8192, (16, HALF))
    QO_gated = v(s_QO, 0, 1024, (2, HALF))
    QO_gu = v(s_QO, 1024, 1024, (2, HALF))
    QO_lAT = v(s_QO, 4096, 2048, (16, 128))
    QO_lBT = v(s_QO, 6144, 2048, (16, 128))

    ACC_acc = v(s_ACC, 0, 8192, (4, D))
    ACC_wkv = v(s_ACC, 0, 8192, (16, 512))

    BIG_xtA = v(s_BIG, 0, 4096, (2, D))
    BIG_xtB = v(s_BIG, 4096, 4096, (2, D))
    BIG_xnt = v(s_BIG, 8192, 4096, (2, D))
    BIG_xntB = v(s_QO, 4096, 4096, (2, D))
    BIG_xnf = v(s_QO, 0, 4096, (16, 256))
    BIG_probs = [v(s_BIG, 0, 2048, None), v(s_BIG, 2048, 2048, None)]
    BIG_probsT = v(s_BIG, 4096, 2048, None)
    BIG_wcb = [v(s_BIG, 4096, 4096, (16, 256)),
               v(s_BIG, 8192, 4096, (16, 256))]
    BIG_qstage = [v(s_BIG, 12288, 512, None), v(s_BIG, 12800, 512, None)]
    BIG_wnat = v(s_BIG, 0, 4096, (2, D))
    BIG_wnat2 = v(s_BIG, 0, 4096, (16, 256))
    BIG_dnat = v(s_BIG, 4096, 2048, (8, 256))
    BIG_dnat2 = v(s_BIG, 4096, 2048, None)
    BIG_wT = v(s_BIG, 6144, 4096, (16, 256))
    BIG_dT = v(s_BIG, 10240, 4096, (2, D))
    BIG_lstA = [s_BIG[0:16, 6144:8192],
                s_BIG[0:16, 8192:10240]]
    BIG_stBn = [v(s_BIG, 10240, 256, (16, 16)),
                v(s_BIG, 10496, 256, (16, 16))]
    BIG_stBr = s_BIG[0:16, 10752:12800]
    BIG_obuf = [v(s_BIG, 0, 2048, None), v(s_BIG, 2048, 2048, None)]
    BIG_xfin = [v(s_BIG, 4096, 2048, None), v(s_BIG, 6144, 2048, None)]
    BIG_xf4 = [v(s_BIG, 4096, 2048, None), v(s_BIG, 6144, 2048, None),
               v(s_BIG, 0, 2048, None), v(s_BIG, 2048, 2048, None)]

    SM_ell = s_SM[:, 0:16]
    SM_ei = s_SM[:, 16:32]
    SM_gstat = v(s_SM, 32, 64, (4, 16))
    SM_gsc = v(s_SM, 96, 32, (4, 8))
    SM_gst = v(s_SM, 128, 32, (4, 8))
    SM_stats = v(s_SM, 160, 32, (4, 8))
    SM_sgst = s_SM[0:8, 192:704]
    SM_lr = s_SM[:, 704:1216]
    SM_wfm = s_SM[0:8, 1216:1728]
    SM_browq = s_SM[0:1, 2048:4096]
    SM_browo = s_SM[32:33, 2048:4096]

    def vh(arena, lo, sz, shape=None, p0=0, p1=128):
        ap = arena[p0:p1, lo:lo + sz].bitcast(F16)
        if shape is not None:
            ap = ap.rearrange("p (a b) -> p a b", a=shape[0], b=shape[1])
        return ap

    XN_h = vh(s_XN, 0, 4096, (16, HALF))
    XN_l = vh(s_XN, 4096, 4096, (16, HALF))
    G_h = vh(s_Gp, 0, 64, (16, 8))
    G_l = vh(s_Gp, 64, 64, (16, 8))
    Q_wst = v(s_BIG, 0, 4096, (16, 256))
    Q_wst2 = v(s_ACC, 0, 4096, (16, 256))
    Q_wh = [vh(s_BIG, 4096, 2048, (16, 256)), vh(s_BIG, 8192, 2048, (16, 256))]
    Q_wl = [vh(s_BIG, 6144, 2048, (16, 256)), vh(s_BIG, 10240, 2048, (16, 256))]
    WT_h = vh(s_BIG, 6144, 2048, (16, 256))
    WT_l = vh(s_BIG, 8192, 2048, (16, 256))
    DT_h = vh(s_BIG, 10240, 2048, (2, D))
    DT_l = vh(s_BIG, 12288, 2048, (2, D))
    QO_gt = v(s_QO, 0, 1024, (2, HALF))
    QO_gh = vh(s_QO, 1024, 512, (2, HALF))
    QO_gl = vh(s_QO, 1536, 512, (2, HALF))
    QO_guf = v(s_QO, 2048, 1024, (2, HALF))
    LAT_h = vh(s_QO, 4096, 1024, (16, 128))
    LAT_l = vh(s_QO, 5120, 1024, (16, 128))
    LBT_h = vh(s_QO, 6144, 1024, (16, 128))
    LBT_l = vh(s_QO, 7168, 1024, (16, 128))
    SBR_h = vh(s_BIG, 10752, 1024, None, 0, 16)
    SBR_l = vh(s_BIG, 11776, 1024, None, 0, 16)
    LR_h = vh(s_SM, 192, 256, None)
    LR_l = vh(s_SM, 448, 256, None)

    ps = [nc.alloc_psum_tensor(f"pb{i}", [128, 512], F32) for i in range(8)]

    import contextlib
    stack = contextlib.ExitStack()
    chans = []

    def mkch(name):
        ch = Ch(stack.enter_context(nc.semaphore(name)), name)
        chans.append(ch)
        return ch

    c_dx = mkch("dx")
    c_dw = mkch("dw")
    c_dg = mkch("dg")
    c_st = mkch("st")
    c_t = mkch("t")
    c_v = mkch("v")
    c_s = mkch("s")
    c_p = mkch("p")
    c_st2 = mkch("st2")

    P = Prog(chans)
    bw = [Mark() for _ in range(8)]
    br = [Mark() for _ in range(8)]

    def dma(eng_name, ch, dst, src, waits=(), tag=""):
        def fn(eng, dst=dst, src=src):
            return eng.dma_start(dst, src)
        P.emit(eng_name, fn, waits=waits, inc=ch, inc_amt=16, tag=tag)
        return Mark(ch, ch.n)

    def mm(bank_i, triples, waits=(), tag=""):
        ws = list(br[bank_i].waits()) + list(waits)
        def fn(eng, triples=triples):
            inst = None
            for lhsT, rhs, out, st_f, sp_f in triples:
                inst = eng.matmul(out, lhsT, rhs, start=st_f, stop=sp_f)
            return inst
        P.emit("tensor", fn, waits=ws, inc=c_t, tag=tag)
        bw[bank_i] = Mark(c_t, c_t.n)
        return bw[bank_i]

    def trg(bank_i, items, waits=(), tag=""):
        ws = list(br[bank_i].waits()) + list(waits)
        def fn(eng, items=items):
            inst = None
            for out, in_, idn in items:
                inst = eng.transpose(out, in_, idn)
            return inst
        P.emit("tensor", fn, waits=ws, inc=c_t, tag=tag)
        bw[bank_i] = Mark(c_t, c_t.n)
        return bw[bank_i]

    def evac(bank_i, eng_name, fn, waits=(), tag=""):
        ch = c_v if eng_name == "vector" else c_s
        ws = list(bw[bank_i].waits()) + list(waits)
        P.emit(eng_name, fn, waits=ws, inc=ch, tag=tag)
        br[bank_i] = Mark(ch, ch.n)
        return br[bank_i]

    dma("sync", c_dx, s_ident[:, :], d_ident, tag="ident")
    dma("sync", c_dx, s_mask[:, :], d_mask, tag="mask")
    dma("sync", c_dx, s_esel16[:, :], d_esel16, tag="esel16")
    dma("sync", c_dx, s_bk[:, :], d_bk, tag="bk")
    dma("sync", c_dx, s_bvrep[:, :], d_bvrep, tag="bvrep")
    dma("sync", c_dx, s_G[:, :, :],
        d_G.rearrange("(k p) e -> p k e", p=128), tag="G")
    dma("sync", c_dx, s_ones[:, :], d_ones, tag="ones")
    dma("sync", c_dx, s_eps[:, :], d_eps, tag="eps")
    dma("sync", c_dx, SM_browq, d_bqbo[0:1, :], tag="browq")
    dma("sync", c_dx, SM_browo, d_bqbo[1:2, :], tag="browo")
    m_consts = Mark(c_dx, c_dx.n)
    P.emit("scalar", lambda eng: eng.copy(G_h[:, :, :], s_G[:, :, :]),
           waits=m_consts.waits(), inc=c_s, tag="Gh")
    mk_Gh = Mark(c_s, c_s.n)
    P.emit("vector", lambda eng: eng.tensor_sub(G_l[:, :, :], s_G[:, :, :],
                                                G_h[:, :, :]),
           waits=mk_Gh.waits(), inc=c_v, tag="Gl")
    mk_Gp = Mark(c_v, c_v.n)
    m_wkv = dma("sync", c_dw, ACC_wkv[:, :, 0:256],
                d_wk.rearrange("(k p) d -> p k d", p=128), tag="wk")
    m_wkv = dma("sync", c_dw, ACC_wkv[:, :, 256:512],
                d_wv.rearrange("(k p) d -> p k d", p=128), tag="wv")

    def emit_ln(xt_view, xnt_view, nchunk, dep, tag, guard=None, st_off=0):
        if guard is None:
            gw = []
        elif isinstance(guard, list):
            gw = [w for g in guard for w in g.waits()]
        else:
            gw = guard.waits()
        for cI in range(nchunk):
            xi = xt_view[:, cI, :]
            st = SM_stats[:, st_off + cI, :]
            P.emit("scalar",
                   lambda eng, xi=xi, st=st, cI=cI: eng.activation(
                       xnt_view[:, cI, :], xi, ACT.Square, accum_out=st[:, 1:2]),
                   waits=list(dep.waits()) + (gw if cI == 0 else []),
                   inc=c_s, tag=f"{tag}sq{cI}")
            mk_sq = Mark(c_s, c_s.n)
            P.emit("vector",
                   lambda eng, xi=xi, st=st: eng.reduce_sum(st[:, 0:1], xi, axis=AX),
                   waits=dep.waits(), inc=c_v, tag=f"{tag}sum{cI}")
            mk_a = Mark(c_v, c_v.n)
            P.emit("vector",
                   lambda eng, st=st: eng.tensor_scalar_mul(
                       st[:, 2:4], st[:, 0:2], 1.0 / D),
                   waits=[(c_s, mk_sq.val), (c_v, mk_a.val)],
                   inc=c_v, tag=f"{tag}m{cI}")
            mk_b = Mark(c_v, c_v.n)
            P.emit("vector",
                   lambda eng, st=st: eng.tensor_mul(st[:, 4:5], st[:, 2:3], st[:, 2:3]),
                   waits=[(c_v, mk_b.val)], inc=c_v, tag=f"{tag}mm{cI}")
            mk_c = Mark(c_v, c_v.n)
            P.emit("vector",
                   lambda eng, st=st: eng.tensor_sub(st[:, 4:5], st[:, 3:4], st[:, 4:5]),
                   waits=[(c_v, mk_c.val)], inc=c_v, tag=f"{tag}var{cI}")
            mk_var = Mark(c_v, c_v.n)
            P.emit("scalar",
                   lambda eng, st=st: eng.activation(st[:, 5:6], st[:, 4:5],
                                                     ACT.Ln, bias=s_eps[:, :]),
                   waits=[(c_v, mk_var.val)], inc=c_s, tag=f"{tag}ln{cI}")
            mk_ln2 = Mark(c_s, c_s.n)
            P.emit("scalar",
                   lambda eng, st=st: eng.activation(st[:, 6:7], st[:, 5:6],
                                                     ACT.Exp, scale=-0.5),
                   waits=[(c_s, mk_ln2.val)], inc=c_s, tag=f"{tag}rstd{cI}")
            mk_rstd = Mark(c_s, c_s.n)
            P.emit("vector",
                   lambda eng, xi=xi, st=st, cI=cI: eng.tensor_scalar(
                       xnt_view[:, cI, :], xi, st[:, 2:3], st[:, 6:7],
                       OP.subtract, OP.mult),
                   waits=[(c_s, mk_rstd.val), (c_v, mk_var.val)],
                   inc=c_v, tag=f"{tag}xn{cI}")
        return Mark(c_v, c_v.n)

    xt_free = [Mark(), Mark()]
    xnt_guard = Mark()
    xts = [BIG_xtA, BIG_xtB]
    xnts = [BIG_xnt, BIG_xntB]
    src0 = d_xseq.rearrange("(s c p) d -> s p c d", p=128, c=2)[0]
    m_x0 = dma("sync", c_dx, xts[0][:, :, :], src0, waits=[], tag="ab1x0")
    mk_lns = {0: emit_ln(xts[0], xnts[0], 2, m_x0, "a1l0_", guard=None)}
    xt_free[0] = mk_lns[0]
    for sti in range(8):
        buf = sti % 2
        xnt_cur = xnts[buf]
        mk_ln = mk_lns[sti]
        if DBG and sti == 1:
            mt1 = dma("sync", c_dx, d_dbgLN, s_BIG[:, 0:12288],
                      waits=mk_ln.waits(), tag="dbgLN")
            mt2 = dma("sync", c_dx, d_dbgST, s_SM[:, :], waits=mk_ln.waits(),
                      tag="dbgST")
            xt_free[buf] = Mark(c_dx, mt2.val)
        mk_fm = Mark()
        for i0 in range(0, 32, 4):
            cc, k0 = divmod(i0, 16)
            bank = (4, 5, 7)[(i0 // 4) % 3]
            items = [(ps[bank][:, 128 * jj:128 * (jj + 1)],
                      xnt_cur[:, cc, 128 * (k0 + jj):128 * (k0 + jj + 1)],
                      s_ident[:, :]) for jj in range(4)]
            trg(bank, items, waits=mk_ln.waits(), tag=f"a1T{sti}_{i0}")
            def ev(eng, cc=cc, k0=k0, bank=bank):
                return eng.tensor_copy(
                    BIG_xnf[:, k0:k0 + 4, 128 * cc:128 * (cc + 1)],
                    ps[bank][:, 0:512])
            evac(bank, "vector", ev, tag=f"a1Tev{sti}_{i0}")
            mk_fm = Mark(c_v, c_v.n)
        xnt_guard = Mark(c_t, c_t.n)
        if sti + 1 < 8:
            nbuf = (sti + 1) % 2
            srcn = d_xseq.rearrange("(s c p) d -> s p c d", p=128, c=2)[sti + 1]
            m_xn2 = dma("sync", c_dx, xts[nbuf][:, :, :], srcn,
                        waits=xt_free[nbuf].waits(), tag=f"ab1x{sti + 1}")
            mk_lns[sti + 1] = emit_ln(xts[nbuf], xnts[nbuf], 2, m_xn2,
                                      f"a1l{sti + 1}_", guard=xnt_guard)
            xt_free[nbuf] = mk_lns[sti + 1]
        for dcI in range(2):
            bank = dcI
            triples = [(ACC_wkv[:, k, 128 * dcI:128 * (dcI + 1)],
                        BIG_xnf[:, k, :].rearrange("p a b -> p (a b)") if False
                        else BIG_xnf[:, k, 0:256],
                        ps[bank][:, 0:256], k == 0, k == 15) for k in range(16)]
            mm(bank, triples,
               waits=list(mk_fm.waits()) + list(m_wkv.waits()), tag=f"a1K{sti}_{dcI}")
            def evK(eng, bank=bank, dcI=dcI, sti=sti):
                return eng.activation(s_K[:, dcI, 256 * sti:256 * (sti + 1)],
                                      ps[bank][:, 0:256], ACT.Identity,
                                      bias=s_bk[:, dcI:dcI + 1])
            evac(bank, "scalar", evK, waits=m_consts.waits(), tag=f"a1Ke{sti}_{dcI}")
        for ss in range(2):
            bank = 2 + ss
            triples = [(BIG_xnf[:, k, 128 * ss:128 * (ss + 1)],
                        ACC_wkv[:, k, 256:512],
                        ps[bank][:, 0:256], k == 0, k == 15) for k in range(16)]
            mm(bank, triples, waits=list(mk_fm.waits()) + list(m_wkv.waits()),
               tag=f"a1V{sti}_{ss}")
            def evV(eng, bank=bank, sti=sti, ss=ss):
                return eng.tensor_add(s_V[:, 2 * sti + ss, :],
                                      ps[bank][:, 0:256], s_bvrep[:, :])
            evac(bank, "vector", evV, waits=m_consts.waits(), tag=f"a1Ve{sti}_{ss}")
    mark_K = Mark(c_s, c_s.n)
    mark_V = Mark(c_v, c_v.n)
    if DBG:
        mk_ = dma("sync", c_dx, d_dbgK, s_K[:, :, :].rearrange("p a b -> p (a b)"),
                  waits=mark_K.waits(), tag="dbgK")
        mark_K = Mark(c_dx, mk_.val)
        mv_ = dma("sync", c_dx, d_dbgV, s_V[:, :, :].rearrange("p a b -> p (a b)"),
                  waits=mark_V.waits(), tag="dbgV")
        mark_V = Mark(c_dx, mv_.val)
    mark_ab1_t = Mark(c_t, c_t.n)

    acc_free = Mark()
    acc_free = mark_ab1_t
    qo_free = mark_ab1_t
    xn_free = Mark()
    big_free = Mark(c_v, c_v.n)
    big_free2 = Mark()
    big_free_b = Mark()
    big_free_b2 = Mark()

    pvq_r = [Mark() for _ in range(4)]

    store_total = 0

    for h2 in range(2):
        mk_fm = Mark()
        mk_lns2 = {}
        for st2 in range(2):
            buf = (st2 + h2) % 2
            src = d_xown.rearrange("(s c p) d -> s p c d", p=128, c=2)[2 * h2 + st2]
            if h2 == 0:
                extra = list(big_free.waits()) if st2 == 0 else []
            else:
                extra = (list(big_free_b.waits()) + list(big_free_b2.waits())
                         if buf == 1 else
                         list(big_free.waits()) + list(big_free2.waits()))
            m_x = dma("sync", c_dx, xts[buf][:, :, :], src,
                      waits=list(xt_free[buf].waits()) + extra,
                      tag=f"h{h2}x{st2}")
            xnt_cur = BIG_xnt if buf == 0 else BIG_xntB
            mk_lns2[st2] = emit_ln(xts[buf], xnt_cur, 2, m_x, f"h{h2}l{st2}_",
                                   guard=[xnt_guard, qo_free], st_off=2 * st2)
            xt_free[buf] = mk_lns2[st2]
        for st2 in range(2):
            buf = (st2 + h2) % 2
            xnt_cur = BIG_xnt if buf == 0 else BIG_xntB
            mk_ln = mk_lns2[st2]
            for i0 in range(0, 32, 4):
                cc, k0 = divmod(i0, 16)
                bank = (4, 5, 7)[(i0 // 4) % 3]
                items = [(ps[bank][:, 128 * jj:128 * (jj + 1)],
                          xnt_cur[:, cc, 128 * (k0 + jj):128 * (k0 + jj + 1)],
                          s_ident[:, :]) for jj in range(4)]
                trg(bank, items,
                    waits=list(mk_ln.waits()) + (list(xn_free.waits()) if (st2 == 0 and i0 == 0) else []),
                    tag=f"h{h2}T{st2}_{i0}")
                sl = slice(256 * st2 + 128 * cc, 256 * st2 + 128 * (cc + 1))
                def evh(eng, k0=k0, bank=bank, sl=sl):
                    return eng.copy(XN_h[:, k0:k0 + 4, sl], ps[bank][:, 0:512])
                P.emit("scalar", evh, waits=bw[bank].waits(), inc=c_s,
                       tag=f"h{h2}Th{st2}_{i0}")
                mk_h = Mark(c_s, c_s.n)
                def evl(eng, k0=k0, bank=bank, sl=sl):
                    return eng.tensor_sub(XN_l[:, k0:k0 + 4, sl],
                                          ps[bank][:, 0:512],
                                          XN_h[:, k0:k0 + 4, sl])
                evac(bank, "vector", evl, waits=mk_h.waits(),
                     tag=f"h{h2}Te{st2}_{i0}")
                mk_fm = Mark(c_v, c_v.n)
            xnt_guard = Mark(c_t, c_t.n)
        mark_xn = mk_fm
        if DBG and h2 == 0:
            mxn = dma("sync", c_dx, d_dbgXN,
                      s_XN[:, :, :].rearrange("p a b -> p (a b)"),
                      waits=mark_xn.waits(), tag="dbgXN")
            mark_xn = Mark(c_dx, mxn.val)
        mark_ab2_t = Mark(c_t, c_t.n)

        wsts = [Q_wst, Q_wst2]
        wst_free2 = [xt_free[0], acc_free]
        pair_free = [xt_free[1], mark_ab2_t]
        qst_free = [Mark(), Mark()]

        def q_split(fb):
            wbuf = fb % 2
            stg_w = wsts[wbuf]
            src = d_wq.rearrange("(k p) d -> p k d", p=128)[:, :, 256 * fb:256 * (fb + 1)]
            m_w = dma("sync", c_dw, stg_w[:, :, :], src,
                      waits=list(wst_free2[wbuf].waits()),
                      tag=f"h{h2}qw{fb}")
            P.emit("scalar",
                   lambda eng, wbuf=wbuf, stg_w=stg_w: eng.copy(
                       Q_wh[wbuf][:, :, :], stg_w[:, :, :]),
                   waits=list(m_w.waits()) + list(pair_free[wbuf].waits()),
                   inc=c_s, tag=f"h{h2}qsh{fb}")
            mk_wh = Mark(c_s, c_s.n)
            P.emit("vector",
                   lambda eng, wbuf=wbuf, stg_w=stg_w: eng.tensor_sub(
                       Q_wl[wbuf][:, :, :], stg_w[:, :, :], Q_wh[wbuf][:, :, :]),
                   waits=mk_wh.waits(), inc=c_v, tag=f"h{h2}qsl{fb}")
            mk_wp = Mark(c_v, c_v.n)
            wst_free2[wbuf] = mk_wp
            return mk_wp

        mk_wps = {0: q_split(0)}
        for fb in range(8):
            wbuf = fb % 2
            mk_wp = mk_wps[fb]
            if fb + 1 < 8:
                mk_wps[fb + 1] = q_split(fb + 1)
            for tt in range(4):
                bank = tt
                triples = []
                for k in range(16):
                    xh = XN_h[:, k, 128 * tt:128 * (tt + 1)]
                    xl = XN_l[:, k, 128 * tt:128 * (tt + 1)]
                    triples.append((xh, Q_wh[wbuf][:, k, :], ps[bank][:, 0:256],
                                    k == 0, False))
                    triples.append((xh, Q_wl[wbuf][:, k, :], ps[bank][:, 0:256],
                                    False, False))
                    triples.append((xl, Q_wh[wbuf][:, k, :], ps[bank][:, 0:256],
                                    False, False))
                triples.append((s_ones[0:1, :], SM_browq[:, 256 * fb:256 * (fb + 1)],
                                ps[bank][:, 0:256], False, True))
                mm(bank, triples,
                   waits=list(mk_wp.waits()) + (list(mark_xn.waits()) if (fb == 0) else []) +
                         (list(m_consts.waits()) if (h2 == 0 and fb == 0) else []),
                   tag=f"h{h2}qm{fb}_{tt}")
                pair_free[wbuf] = Mark(c_t, c_t.n)
                sbuf_stage = BIG_qstage[tt % 2]
                def evq(eng, bank=bank, stg=sbuf_stage):
                    return eng.tensor_copy(stg[:, 0:256], ps[bank][:, 0:256])
                evac(bank, "vector", evq,
                     waits=qst_free[tt % 2].waits(),
                     tag=f"h{h2}qe{fb}_{tt}")
                mk_stage = Mark(c_v, c_v.n)
                tbank = 4 + tt % 2
                items = [(ps[tbank][:, 128 * jj:128 * (jj + 1)],
                          sbuf_stage[:, 128 * jj:128 * (jj + 1)], s_ident[:, :])
                         for jj in range(2)]
                trg(tbank, items, waits=mk_stage.waits(), tag=f"h{h2}qT{fb}_{tt}")
                qst_free[tt % 2] = Mark(c_t, c_t.n)
                def evqs(eng, tbank=tbank, fb=fb, tt=tt):
                    return eng.activation(
                        QO_Q[:, 2 * fb:2 * fb + 2, 128 * tt:128 * (tt + 1)],
                        ps[tbank][:, 0:256], ACT.Copy, scale=ALPHA)
                evac(tbank, "scalar", evqs,
                     waits=(qo_free.waits() if (fb == 0 and tt == 0) else []),
                     tag=f"h{h2}qs{fb}_{tt}")
        wcb_free = [pair_free[0], pair_free[1]]
        wst2_free = wst_free2[1]
        mark_Q = Mark(c_s, c_s.n)
        if DBG:
            mq = dma("sync", c_dx, d_dbgQ[h2], s_QO[:, :],
                     waits=mark_Q.waits(), tag=f"dbgQ{h2}")
            mark_Q = Mark(c_dx, mq.val)

        heads = [(tt, h) for tt in range(4) for h in range(NH)]
        state = {}

        def at_scores(tt, h, first):
            j = 4 * h2 + tt
            c = j + 1
            nb = (c + 1) // 2
            ws = []
            for bk_i in range(nb):
                ws += br[bk_i].waits()
            if first:
                ws += list(mark_K.waits()) + list(mark_V.waits()) + list(mark_Q.waits())
            def fn(eng, tt=tt, h=h, c=c):
                inst = None
                lhsT = QO_Q[:, h, 128 * tt:128 * (tt + 1)]
                for i in range(c):
                    bank, halfc = divmod(i, 2)
                    out = ps[bank][:, 256 * halfc:256 * (halfc + 1)]
                    inst = eng.matmul(out, lhsT, s_K[:, h // 8, 256 * i:256 * (i + 1)],
                                      start=True, stop=True)
                return inst
            P.emit("tensor", fn, waits=ws, inc=c_t, tag=f"h{h2}sc{tt}_{h}")
            mk_sc = Mark(c_t, c_t.n)
            for bk_i in range(nb):
                bw[bk_i] = mk_sc
            return mk_sc

        def at_evmax(tt, h, mk_sc, first):
            j = 4 * h2 + tt
            c = j + 1
            nb = (c + 1) // 2
            pb_ = BIG_probs[h % 2]
            bank_d, half_d = divmod(c - 1, 2)
            def fev(eng, c=c, nb=nb, pb_=pb_, bank_d=bank_d, half_d=half_d):
                for bk_i in range(nb):
                    chunks = [i for i in (2 * bk_i, 2 * bk_i + 1)
                              if i < c and i != c - 1]
                    if len(chunks) == 2:
                        eng.tensor_copy(pb_[:, 512 * bk_i:512 * bk_i + 512],
                                        ps[bk_i][:, 0:512])
                    elif len(chunks) == 1:
                        i = chunks[0]
                        eng.tensor_copy(pb_[:, 256 * i:256 * (i + 1)],
                                        ps[bk_i][:, 256 * (i % 2):256 * (i % 2) + 256])
                return eng.tensor_add(pb_[:, 256 * (c - 1):256 * c],
                                      ps[bank_d][:, 256 * half_d:256 * (half_d + 1)],
                                      s_mask[:, :])
            ws_e = list(mk_sc.waits())
            if first:
                ws_e += list(m_consts.waits())
            prev2 = state.get("probs_free", {}).get(h % 2)
            if prev2 is not None:
                ws_e += prev2.waits()
            P.emit("vector", fev, waits=ws_e, inc=c_v, tag=f"h{h2}se{tt}_{h}")
            mk_ev = Mark(c_v, c_v.n)
            for bk_i in range(nb):
                br[bk_i] = mk_ev
            return mk_ev

        def at_max(tt, h, mk_ev):
            c = 4 * h2 + tt + 1
            pb_ = BIG_probs[h % 2]
            cm = 1 + (h % 2)
            P.emit("vector",
                   lambda eng, c=c, pb_=pb_, cm=cm: eng.reduce_max(
                       SM_ei[:, cm:cm + 1], pb_[:, 0:256 * c], axis=AX, negate=True),
                   waits=mk_ev.waits(), inc=c_v, tag=f"h{h2}mx{tt}_{h}")
            return Mark(c_v, c_v.n)

        def at_exp(tt, h, mk_max):
            c = 4 * h2 + tt + 1
            pb_ = BIG_probs[h % 2]
            cm = 1 + (h % 2)
            def fne(eng, c=c, pb_=pb_, h=h, cm=cm):
                return eng.activation(pb_[:, 0:256 * c], pb_[:, 0:256 * c],
                                      ACT.Exp, bias=SM_ei[:, cm:cm + 1],
                                      accum_out=SM_ell[:, h:h + 1])
            P.emit("scalar", fne, waits=mk_max.waits(), inc=c_s, tag=f"h{h2}ex{tt}_{h}")
            return Mark(c_s, c_s.n)

        def at_norm(tt, h, mk_exp):
            c = 4 * h2 + tt + 1
            pb_ = BIG_probs[h % 2]
            ci = 4 + (h % 2)
            P.emit("vector",
                   lambda eng, h=h, ci=ci: eng.reciprocal(SM_ei[:, ci:ci + 1],
                                                          SM_ell[:, h:h + 1]),
                   waits=mk_exp.waits(), inc=c_v, tag=f"h{h2}nm2{tt}_{h}")
            mk_n2 = Mark(c_v, c_v.n)
            P.emit("scalar",
                   lambda eng, c=c, pb_=pb_, ci=ci: eng.activation(
                       pb_[:, 0:256 * c], pb_[:, 0:256 * c], ACT.Identity,
                       scale=SM_ei[:, ci:ci + 1]),
                   waits=mk_n2.waits(), inc=c_s, tag=f"h{h2}nm3{tt}_{h}")
            return Mark(c_s, c_s.n)

        def at_trg(tt, h, mk_norm, lo_g, hi_g):
            c = 4 * h2 + tt + 1
            pb_ = BIG_probs[h % 2]
            nsub = 2 * c
            grps = []
            for gi in range((nsub + 3) // 4)[lo_g:hi_g]:
                i0 = 4 * gi
                grp = list(range(i0, min(i0 + 4, nsub)))
                bank = (4, 5, 7)[gi % 3]
                items = [(ps[bank][:, 128 * jj:128 * (jj + 1)],
                          pb_[:, 128 * ii:128 * (ii + 1)], s_ident[:, :])
                         for jj, ii in enumerate(grp)]
                trg(bank, items, waits=mk_norm.waits() if i0 == 0 else [],
                    tag=f"h{h2}pT{tt}_{h}_{i0}")
                grps.append((grp, bank))
            return grps

        def at_evp(tt, h, grps, first_grp):
            mk_ev = Mark()
            prev_pv = state.get("pv_mark")
            for gi, (grp, bank) in enumerate(grps):
                ws_ev = (prev_pv.waits()
                         if (gi == 0 and first_grp and prev_pv is not None) else [])
                def evp(eng, grp=grp, bank=bank):
                    w = 128 * len(grp)
                    return eng.tensor_copy(BIG_probsT[:, 128 * grp[0]:128 * grp[0] + w],
                                           ps[bank][:, 0:w])
                evac(bank, "vector", evp, waits=ws_ev, tag=f"h{h2}pE{tt}_{h}_{grp[0]}")
                mk_ev = Mark(c_v, c_v.n)
            return mk_ev

        def at_trg_evp(tt, h, mk_norm):
            c = 4 * h2 + tt + 1
            ngrp = (2 * c + 3) // 4
            g_a = at_trg(tt, h, mk_norm, 0, min(3, ngrp))
            ev1 = at_evp(tt, h, g_a[0:1], first_grp=True)
            g_b = at_trg(tt, h, mk_norm, 3, ngrp)
            state.setdefault("probs_free", {})[h % 2] = Mark(c_t, c_t.n)
            ev2 = at_evp(tt, h, g_a[1:] + g_b, first_grp=False)
            return ev2 if (len(g_a) + len(g_b)) > 1 else ev1

        def at_pv(tt, h, mk_ev):
            c = 4 * h2 + tt + 1
            qr = h % 4
            def fpv(eng, c=c, h=h, qr=qr):
                inst = None
                out = ps[6][:, 128 * qr:128 * (qr + 1)]
                for i2 in range(2 * c):
                    inst = eng.matmul(out, s_V[:, i2, 128 * (h // 8):128 * (h // 8 + 1)],
                                      BIG_probsT[:, 128 * i2:128 * (i2 + 1)],
                                      start=(i2 == 0), stop=(i2 == 2 * c - 1))
                return inst
            P.emit("tensor", fpv,
                   waits=[(c_v, mk_ev.val)] + pvq_r[qr].waits(),
                   inc=c_t, tag=f"h{h2}pv{tt}_{h}")
            mk_pv = Mark(c_t, c_t.n)
            state["pv_mark"] = mk_pv
            def evo(eng, h=h, tt=tt, qr=qr):
                return eng.copy(QO_Q[:, h, 128 * tt:128 * (tt + 1)],
                                ps[6][:, 128 * qr:128 * (qr + 1)])
            P.emit("scalar", evo, waits=mk_pv.waits(), inc=c_s, tag=f"h{h2}oe{tt}_{h}")
            pvq_r[qr] = Mark(c_s, c_s.n)

        prev = None
        for idx, (tt, h) in enumerate(heads):
            if prev is not None:
                mk_norm = at_norm(prev[0], prev[1], prev[2])
            mk_sc = at_scores(tt, h, first=(idx == 0))
            mk_ev = at_evmax(tt, h, mk_sc, first=(idx == 0))
            if prev is not None:
                mk_evp = at_trg_evp(prev[0], prev[1], mk_norm)
            mk_max = at_max(tt, h, mk_ev)
            mk_exp = at_exp(tt, h, mk_max)
            if prev is not None:
                at_pv(prev[0], prev[1], mk_evp)
            prev = (tt, h, mk_exp)
        mk_norm = at_norm(prev[0], prev[1], prev[2])
        mk_evp = at_trg_evp(prev[0], prev[1], mk_norm)
        at_pv(prev[0], prev[1], mk_evp)
        mark_O = Mark(c_s, c_s.n)
        if DBG:
            mo = dma("sync", c_dx, d_dbgO[h2], s_QO[:, :],
                     waits=mark_O.waits(), tag=f"dbgO{h2}")
            mark_O = Mark(c_dx, mo.val)

        for nqt in range(1):
            pass
        mk_gsc = Mark()
        for gb in range(1):
            triples = []
            for k in range(16):
                triples.append((G_h[:, k, :], XN_h[:, k, 0:512],
                                ps[7][0:8, 0:512], k == 0, False))
                triples.append((G_h[:, k, :], XN_l[:, k, 0:512],
                                ps[7][0:8, 0:512], False, False))
                triples.append((G_l[:, k, :], XN_h[:, k, 0:512],
                                ps[7][0:8, 0:512], False, k == 15))
            mm(7, triples, waits=list(mark_xn.waits()) + list(mk_Gp.waits()),
               tag=f"h{h2}gm")
            def evsg(eng):
                return eng.tensor_copy(SM_sgst, ps[7][0:8, 0:512])
            evac(7, "vector", evsg, tag=f"h{h2}ge")
            mk_sg = Mark(c_v, c_v.n)
            for jj in range(4):
                bank = 4 + jj % 2
                items = [(ps[bank][:, 0:8],
                          SM_sgst[0:8, 128 * jj:128 * (jj + 1)],
                          s_ident[0:8, 0:8])]
                trg(bank, items, waits=mk_sg.waits(), tag=f"h{h2}gT{jj}")
                def evgt(eng, jj=jj, bank=bank):
                    return eng.tensor_copy(SM_gsc[:, jj, :], ps[bank][:, 0:8])
                evac(bank, "vector", evgt, tag=f"h{h2}gTe{jj}")
            mk_gsc = Mark(c_v, c_v.n)
        for sj in range(4):
            sc = SM_gsc[:, sj, :]
            tmp = SM_gst[:, sj, :]
            st = SM_gstat[:, sj, :]
            P.emit("vector",
                   lambda eng, sc=sc, st=st: eng.reduce_max(st[:, 0:1], sc,
                                                            axis=AX, negate=True),
                   waits=mk_gsc.waits(), inc=c_v, tag=f"h{h2}g1_{sj}")
            mk1 = Mark(c_v, c_v.n)
            P.emit("scalar",
                   lambda eng, sc=sc, st=st: eng.activation(
                       sc, sc, ACT.Exp, bias=st[:, 0:1], accum_out=st[:, 1:2]),
                   waits=mk1.waits(), inc=c_s, tag=f"h{h2}g2_{sj}")
            mk2 = Mark(c_s, c_s.n)
            P.emit("vector",
                   lambda eng, st=st: eng.reciprocal(st[:, 2:3], st[:, 1:2]),
                   waits=mk2.waits(), inc=c_v, tag=f"h{h2}g3a_{sj}")
            mg1 = Mark(c_v, c_v.n)
            def fngb(eng, sc=sc, st=st):
                return eng.tensor_scalar_mul(sc, sc, st[:, 2:3])
            steps = [
                lambda eng, sc=sc, st=st: eng.reduce_max(st[:, 3:4], sc, axis=AX),
                lambda eng, sc=sc, tmp=tmp, st=st: eng.tensor_scalar(
                    tmp, sc, st[:, 3:4], -1e30, OP.is_equal, OP.mult),
                lambda eng, sc=sc, tmp=tmp: eng.tensor_add(tmp, sc, tmp),
                lambda eng, tmp=tmp, st=st: eng.reduce_max(st[:, 4:5], tmp, axis=AX),
                lambda eng, sc=sc, tmp=tmp, st=st: eng.tensor_scalar(
                    tmp, sc, st[:, 4:5], 0.0, OP.is_ge, OP.add),
                lambda eng, sc=sc, tmp=tmp: eng.tensor_mul(sc, sc, tmp),
            ]
            mprev = Mark(c_v, c_v.n)
            P.emit("vector", fngb, waits=mg1.waits(), inc=c_v, tag=f"h{h2}g3b_{sj}")
            mprev = Mark(c_v, c_v.n)
            for si, fstep in enumerate(steps):
                P.emit("vector", fstep, waits=mprev.waits(), inc=c_v,
                       tag=f"h{h2}g3_{sj}_{si}")
                mprev = Mark(c_v, c_v.n)
        mk_w = Mark(c_v, c_v.n)
        for sj in range(4):
            bank = 4 + sj % 2
            items = [(ps[bank][0:8, 0:128], SM_gsc[:, sj, :], s_ident[:, :])]
            trg(bank, items, waits=mk_w.waits(), tag=f"h{h2}wT{sj}")
            def evwf(eng, sj=sj, bank=bank):
                return eng.tensor_copy(SM_wfm[0:8, 128 * sj:128 * (sj + 1)],
                                       ps[bank][0:8, 0:128])
            evac(bank, "vector", evwf, tag=f"h{h2}wTe{sj}")
        mk_wfm = Mark(c_v, c_v.n)

        for fb in range(8):
            wbuf = fb % 2
            src = d_wo.rearrange("(k p) d -> p k d", p=128)[:, :, 256 * fb:256 * (fb + 1)]
            m_w = dma("sync", c_dw, BIG_wcb[wbuf][:, :, :], src,
                      waits=list(wcb_free[wbuf].waits()) +
                            (list(mark_O.waits()) if fb < 2 else []),
                      tag=f"h{h2}ow{fb}")
            for tt in range(4):
                bank = tt
                triples = [(QO_Q[:, k, 128 * tt:128 * (tt + 1)],
                            BIG_wcb[wbuf][:, k, :],
                            ps[bank][:, 0:256], k == 0, False) for k in range(16)]
                triples.append((s_ones[32:33, :], SM_browo[:, 256 * fb:256 * (fb + 1)],
                                ps[bank][:, 0:256], False, True))
                ws = list(m_w.waits())
                if fb == 0:
                    ws += list(mark_O.waits())
                if fb == 0 and tt == 0:
                    ws += list(acc_free.waits())
                mm(bank, triples, waits=ws, tag=f"h{h2}om{fb}_{tt}")
                wcb_free[wbuf] = Mark(c_t, c_t.n)
                def evw(eng, bank=bank, fb=fb, tt=tt):
                    return eng.copy(ACC_acc[:, tt, 256 * fb:256 * (fb + 1)],
                                    ps[bank][:, 0:256])
                evac(bank, "scalar", evw, tag=f"h{h2}oe{fb}_{tt}")
        mark_accW = Mark(c_s, c_s.n)


        mk_qo_ffn = Mark(c_t, c_t.n)
        lst_free = [mk_qo_ffn, mk_qo_ffn]
        for e in range(E):
            lbuf = e % 2
            m_l = dma("sync", c_dw, BIG_lstA[lbuf], d_lA[e],
                      waits=lst_free[lbuf].waits(), tag=f"h{h2}lA{e}")
            for i0 in range(0, 16, 4):
                bank = 4 + (i0 // 4) % 2
                items = [(ps[bank][0:128, 128 * jj:128 * jj + 16],
                          BIG_lstA[lbuf][:, 128 * (i0 + jj):128 * (i0 + jj + 1)],
                          s_ident[0:16, 0:16]) for jj in range(4)]
                trg(bank, items, waits=m_l.waits() if i0 == 0 else [],
                    tag=f"h{h2}lAT{e}_{i0}")
                def evlah(eng, e=e, i0=i0, bank=bank):
                    inst = None
                    for jj in range(4):
                        inst = eng.copy(
                            LAT_h[:, i0 + jj, 16 * e:16 * (e + 1)],
                            ps[bank][:, 128 * jj:128 * jj + 16])
                    return inst
                ws = mk_qo_ffn.waits() if (e == 0 and i0 == 0) else []
                P.emit("scalar", evlah, waits=list(bw[bank].waits()) + ws,
                       inc=c_s, tag=f"h{h2}lAh{e}_{i0}")
                mk_lah = Mark(c_s, c_s.n)
                def evlal(eng, e=e, i0=i0, bank=bank):
                    inst = None
                    for jj in range(4):
                        inst = eng.tensor_sub(
                            LAT_l[:, i0 + jj, 16 * e:16 * (e + 1)],
                            ps[bank][:, 128 * jj:128 * jj + 16],
                            LAT_h[:, i0 + jj, 16 * e:16 * (e + 1)])
                    return inst
                evac(bank, "vector", evlal, waits=mk_lah.waits(),
                     tag=f"h{h2}lAe{e}_{i0}")
            lst_free[lbuf] = Mark(c_t, c_t.n)
        mk_lAT = Mark(c_v, c_v.n)
        lstBn_free = [mk_qo_ffn, mk_qo_ffn]
        lstBr_free = mk_qo_ffn
        for e in range(E):
            lbuf = e % 2
            m_l = dma("sync", c_dw, BIG_stBn[lbuf][:, :, :],
                      d_lB[e].rearrange("(k p) r -> p k r", p=128),
                      waits=lstBn_free[lbuf].waits(), tag=f"h{h2}lB{e}")
            mk_ev = Mark()
            for i0 in range(0, 16, 4):
                bank = 4 + (i0 // 4) % 2
                items = [(ps[bank][0:16, 128 * jj:128 * (jj + 1)],
                          BIG_stBn[lbuf][:, i0 + jj, :],
                          s_ident[:, :]) for jj in range(4)]
                trg(bank, items, waits=m_l.waits() if i0 == 0 else [],
                    tag=f"h{h2}lBT{e}_{i0}")
                def evlbh(eng, i0=i0, bank=bank):
                    inst = None
                    for jj in range(4):
                        inst = eng.copy(
                            SBR_h[0:16, 128 * (i0 + jj):128 * (i0 + jj + 1)],
                            ps[bank][0:16, 128 * jj:128 * (jj + 1)])
                    return inst
                ws = lstBr_free.waits() if i0 == 0 else []
                P.emit("scalar", evlbh, waits=list(bw[bank].waits()) + ws,
                       inc=c_s, tag=f"h{h2}lBh{e}_{i0}")
                mk_lbh = Mark(c_s, c_s.n)
                def evlbl(eng, i0=i0, bank=bank):
                    inst = None
                    for jj in range(4):
                        inst = eng.tensor_sub(
                            SBR_l[0:16, 128 * (i0 + jj):128 * (i0 + jj + 1)],
                            ps[bank][0:16, 128 * jj:128 * (jj + 1)],
                            SBR_h[0:16, 128 * (i0 + jj):128 * (i0 + jj + 1)])
                    return inst
                evac(bank, "vector", evlbl, waits=mk_lbh.waits(),
                     tag=f"h{h2}lBe{e}_{i0}")
                mk_ev = Mark(c_v, c_v.n)
            lstBn_free[lbuf] = Mark(c_t, c_t.n)
            def mvlbh(eng, e=e):
                return eng.dma_start(
                    LBT_h[16 * e:16 * (e + 1), :, :].rearrange("p a b -> p (a b)"),
                    SBR_h)
            P.emit("gpsimd", mvlbh, waits=mk_ev.waits(), inc=c_dg, inc_amt=16,
                   tag=f"h{h2}lBmh{e}")
            def mvlbl(eng, e=e):
                return eng.dma_start(
                    LBT_l[16 * e:16 * (e + 1), :, :].rearrange("p a b -> p (a b)"),
                    SBR_l)
            P.emit("gpsimd", mvlbl, waits=mk_ev.waits(), inc=c_dg, inc_amt=16,
                   tag=f"h{h2}lBml{e}")
            lstBr_free = Mark(c_dg, c_dg.n)
        mk_lBT = lstBr_free

        for nq in range(1):
            bank = 0
            triples = []
            for k in range(16):
                triples.append((LAT_h[:, k, :], XN_h[:, k, 0:512],
                                ps[bank][:, 0:512], k == 0, False))
                triples.append((LAT_h[:, k, :], XN_l[:, k, 0:512],
                                ps[bank][:, 0:512], False, False))
                triples.append((LAT_l[:, k, :], XN_h[:, k, 0:512],
                                ps[bank][:, 0:512], False, k == 15))
            mm(bank, triples, waits=list(mk_lAT.waits()) + list(mark_xn.waits()),
               tag=f"h{h2}lrm")
            def evlr(eng, bank=bank):
                return eng.tensor_copy(SM_lr, ps[bank][:, 0:512])
            evac(bank, "vector", evlr, tag=f"h{h2}lre")
            mk_lrev = Mark(c_v, c_v.n)
            bank = 1
            triples = [(s_esel16[:, :], SM_wfm, ps[bank][:, 0:512], True, True)]
            mm(bank, triples, waits=mk_wfm.waits(), tag=f"h{h2}wrm")
            def evls(eng, bank=bank):
                return eng.tensor_mul(SM_lr, SM_lr, ps[bank][:, 0:512])
            evac(bank, "vector", evls, waits=mk_lrev.waits(), tag=f"h{h2}lrs")
        mk_lr = Mark(c_v, c_v.n)
        P.emit("scalar", lambda eng: eng.copy(LR_h, SM_lr),
               waits=mk_lr.waits(), inc=c_s, tag=f"h{h2}lrh")
        mk_lrh = Mark(c_s, c_s.n)
        P.emit("vector", lambda eng: eng.tensor_sub(LR_l, SM_lr, LR_h),
               waits=mk_lrh.waits(), inc=c_v, tag=f"h{h2}lrl")
        mk_lr = Mark(c_v, c_v.n)

        for fb in range(4):
            for tt in range(4):
                bank = (4 * fb + tt) % 4
                triples = [
                    (LR_h[:, 128 * tt:128 * (tt + 1)],
                     LBT_h[:, 4 * fb:4 * (fb + 1), :],
                     ps[bank][:, 0:512], True, False),
                    (LR_h[:, 128 * tt:128 * (tt + 1)],
                     LBT_l[:, 4 * fb:4 * (fb + 1), :],
                     ps[bank][:, 0:512], False, False),
                    (LR_l[:, 128 * tt:128 * (tt + 1)],
                     LBT_h[:, 4 * fb:4 * (fb + 1), :],
                     ps[bank][:, 0:512], False, True)]
                ws = (list(mk_lr.waits()) + list(mk_lBT.waits()) +
                      list(mark_accW.waits())) if (fb == 0 and tt == 0) else []
                mm(bank, triples, waits=ws, tag=f"h{h2}lm{fb}_{tt}")
                def evla2(eng, bank=bank, fb=fb, tt=tt):
                    sl = ACC_acc[:, tt, 512 * fb:512 * (fb + 1)]
                    return eng.tensor_add(sl, sl, ps[bank][:, 0:512])
                evac(bank, "vector", evla2, tag=f"h{h2}la{fb}_{tt}")

        wnat_free = Mark()
        dnat_free = Mark()
        wT_free = Mark()
        dT_free = Mark()
        gated_free = Mark()
        wT_first = [True]

        def dma_gate(e):
            return dma("gpsimd", c_dg, BIG_wnat2[:, :, :],
                       d_gate[e].rearrange("(k p) f -> p k f", p=128),
                       waits=list(wnat_free.waits()) +
                             (list(mark_O.waits()) + list(big_free.waits()) +
                              list(big_free2.waits())
                              if e == 0 else []),
                       tag=f"h{h2}gw{e}")

        def split_gate(e, m_g):
            ws_e = list(mk_lBT.waits()) if wT_first[0] else []
            wT_first[0] = False
            P.emit("scalar",
                   lambda eng: eng.copy(WT_h[:, :, :], BIG_wnat2[:, :, :]),
                   waits=list(m_g.waits()) + list(wT_free.waits()) + ws_e,
                   inc=c_s, tag=f"h{h2}gTh{e}")
            mk_th = Mark(c_s, c_s.n)
            P.emit("vector",
                   lambda eng: eng.tensor_sub(WT_l[:, :, :], BIG_wnat2[:, :, :],
                                              WT_h[:, :, :]),
                   waits=mk_th.waits(), inc=c_v, tag=f"h{h2}gTl{e}")
            return Mark(c_v, c_v.n)

        m_gs = {0: dma_gate(0)}
        mk_gTs = {0: split_gate(0, m_gs[0])}
        for e in range(E):
            m_g = m_gs[e]
            mk_gT = mk_gTs[e]
            wnat_free = mk_gT
            m_d0 = dma("gpsimd", c_dg, BIG_dnat2[:, :],
                       d_down[e].rearrange("(c p) d -> c p d", p=128)[0],
                       waits=list(dnat_free.waits()) +
                             (list(mk_qo_ffn.waits()) if e == 0 else []),
                       tag=f"h{h2}dw{e}_0")
            ws_d = (list(mk_lBT.waits()) if e == 0 else []) + list(dT_free.waits())
            P.emit("scalar",
                   lambda eng: eng.copy(DT_h[:, 0, :], BIG_dnat2[:, :]),
                   waits=list(m_d0.waits()) + ws_d,
                   inc=c_s, tag=f"h{h2}dTh{e}_0")
            mk_dh = Mark(c_s, c_s.n)
            P.emit("vector",
                   lambda eng: eng.tensor_sub(
                       DT_l[:, 0, :], BIG_dnat2[:, :], DT_h[:, 0, :]),
                   waits=mk_dh.waits(), inc=c_v, tag=f"h{h2}dTl{e}_0")
            dnat_free = Mark(c_v, c_v.n)
            m_u = dma("gpsimd", c_dg, BIG_wnat2[:, :, :],
                      d_up[e].rearrange("(k p) f -> p k f", p=128),
                      waits=mk_gT.waits(), tag=f"h{h2}uw{e}")
            m_d1 = dma("gpsimd", c_dg, BIG_dnat2[:, :],
                       d_down[e].rearrange("(c p) d -> c p d", p=128)[1],
                       waits=dnat_free.waits(), tag=f"h{h2}dw{e}_1")
            def mm3w(bank, fc, waits, tag2):
                triples = []
                for k in range(16):
                    wh = WT_h[:, k, 128 * fc:128 * (fc + 1)]
                    wl = WT_l[:, k, 128 * fc:128 * (fc + 1)]
                    triples.append((wh, XN_h[:, k, 0:512], ps[bank][:, 0:512],
                                    k == 0, False))
                    triples.append((wh, XN_l[:, k, 0:512], ps[bank][:, 0:512],
                                    False, False))
                    triples.append((wl, XN_h[:, k, 0:512], ps[bank][:, 0:512],
                                    False, k == 15))
                mm(bank, triples, waits=waits, tag=tag2)

            mk_gmm = []
            mk_uhalf = []
            for fc in range(2):
                bank = fc
                mm3w(bank, fc, mk_gT.waits() if fc == 0 else [],
                     f"h{h2}gm{e}_{fc}")
                mk_gmm.append(Mark(c_t, c_t.n))
                def evgs(eng, bank=bank, fc=fc):
                    return eng.activation(QO_gt[:, fc, :], ps[bank][:, 0:512],
                                          ACT.Silu)
                evac(bank, "scalar", evgs,
                     waits=gated_free.waits() if fc == 0 else [],
                     tag=f"h{h2}gs{e}_{fc}")
                P.emit("scalar",
                       lambda eng, fc=fc: eng.copy(
                           WT_h[:, :, 128 * fc:128 * (fc + 1)],
                           BIG_wnat2[:, :, 128 * fc:128 * (fc + 1)]),
                       waits=list(m_u.waits()) + list(mk_gmm[fc].waits()),
                       inc=c_s, tag=f"h{h2}uTh{e}_{fc}")
                mk_uh = Mark(c_s, c_s.n)
                P.emit("vector",
                       lambda eng, fc=fc: eng.tensor_sub(
                           WT_l[:, :, 128 * fc:128 * (fc + 1)],
                           BIG_wnat2[:, :, 128 * fc:128 * (fc + 1)],
                           WT_h[:, :, 128 * fc:128 * (fc + 1)]),
                       waits=mk_uh.waits(), inc=c_v, tag=f"h{h2}uTl{e}_{fc}")
                mk_uhalf.append(Mark(c_v, c_v.n))
            mk_gsilu = Mark(c_s, c_s.n)
            wnat_free = mk_uhalf[1]
            P.emit("scalar",
                   lambda eng: eng.copy(DT_h[:, 1, :], BIG_dnat2[:, :]),
                   waits=list(m_d1.waits()), inc=c_s, tag=f"h{h2}dTh{e}_1")
            mk_dh = Mark(c_s, c_s.n)
            P.emit("vector",
                   lambda eng: eng.tensor_sub(
                       DT_l[:, 1, :], BIG_dnat2[:, :], DT_h[:, 1, :]),
                   waits=mk_dh.waits(), inc=c_v, tag=f"h{h2}dTl{e}_1")
            dnat_free = Mark(c_v, c_v.n)
            mk_dT = Mark(c_v, c_v.n)
            if e + 1 < E:
                m_gs[e + 1] = dma_gate(e + 1)
            for fc in range(2):
                bank = fc
                mm3w(bank, fc, mk_uhalf[fc].waits(),
                     f"h{h2}um{e}_{fc}")
                def evu(eng, bank=bank, fc=fc):
                    return eng.tensor_copy(QO_guf[:, fc, :], ps[bank][:, 0:512])
                evac(bank, "vector", evu, tag=f"h{h2}ue{e}_{fc}")
                mk_u_ev = Mark(c_v, c_v.n)
                def fgt(eng, fc=fc):
                    return eng.tensor_mul(QO_gt[:, fc, :], QO_gt[:, fc, :],
                                          QO_guf[:, fc, :])
                P.emit("vector", fgt,
                       waits=[(c_s, mk_gsilu.val), (c_v, mk_u_ev.val)], inc=c_v,
                       tag=f"h{h2}gt{e}_{fc}")
                mk_t = Mark(c_v, c_v.n)
                P.emit("scalar",
                       lambda eng, fc=fc: eng.copy(QO_gh[:, fc, :],
                                                   QO_gt[:, fc, :]),
                       waits=mk_t.waits(), inc=c_s, tag=f"h{h2}gth{e}_{fc}")
                mk_th2 = Mark(c_s, c_s.n)
                P.emit("vector",
                       lambda eng, fc=fc: eng.tensor_sub(QO_gl[:, fc, :],
                                                         QO_gt[:, fc, :],
                                                         QO_gh[:, fc, :]),
                       waits=mk_th2.waits(), inc=c_v, tag=f"h{h2}gtl{e}_{fc}")
            mk_gated = Mark(c_v, c_v.n)
            wT_free = Mark(c_t, c_t.n)
            if e + 1 < E:
                mk_gTs[e + 1] = split_gate(e + 1, m_gs[e + 1])

            for fb in range(4):
                for tt in range(4):
                    bank = (4 * fb + tt) % 4
                    triples = []
                    for fc in range(2):
                        gh = QO_gh[:, fc, 128 * tt:128 * (tt + 1)]
                        gl = QO_gl[:, fc, 128 * tt:128 * (tt + 1)]
                        dh = DT_h[:, fc, 512 * fb:512 * (fb + 1)]
                        dl = DT_l[:, fc, 512 * fb:512 * (fb + 1)]
                        triples.append((gh, dh, ps[bank][:, 0:512],
                                        fc == 0, False))
                        triples.append((gh, dl, ps[bank][:, 0:512],
                                        False, False))
                        triples.append((gl, dh, ps[bank][:, 0:512],
                                        False, fc == 1))
                    ws = []
                    if fb == 0 and tt == 0:
                        ws = list(mk_dT.waits()) + list(mk_gated.waits())
                        if e == 0:
                            ws += list(mark_accW.waits())
                    mm(bank, triples, waits=ws, tag=f"h{h2}dm{e}_{fb}_{tt}")
                    def evda(eng, bank=bank, fb=fb, tt=tt, e=e):
                        sl = ACC_acc[:, tt, 512 * fb:512 * (fb + 1)]
                        return eng.scalar_tensor_tensor(
                            sl, ps[bank][:, 0:512], SM_gsc[:, tt, e:e + 1], sl,
                            OP.mult, OP.add)
                    evac(bank, "vector", evda, tag=f"h{h2}da{e}_{fb}_{tt}")
            gated_free = Mark(c_t, c_t.n)
            dT_free = Mark(c_t, c_t.n)
        mark_acc = Mark(c_v, c_v.n)
        qo_free = Mark(c_t, c_t.n)
        big_free_ffn = Mark(c_t, c_t.n)

        xf_gate = [dnat_free, wT_free, wnat_free, wnat_free]
        m_xs = []
        for tt in range(4):
            m_xs.append(dma("sync", c_dx, BIG_xf4[tt],
                            d_xown[:, :].rearrange("(q p) d -> q p d", p=128)[4 * h2 + tt],
                            waits=xf_gate[tt].waits(), tag=f"h{h2}fx{tt}"))
        st_m = []
        for tt in range(4):
            def fadd(eng, tt=tt):
                return eng.tensor_add(BIG_xf4[tt], ACC_acc[:, tt, :],
                                      BIG_xf4[tt])
            P.emit("vector", fadd,
                   waits=list(m_xs[tt].waits()) + (list(mark_acc.waits()) if tt == 0 else []),
                   inc=c_v, tag=f"h{h2}fa{tt}")
            mk_ob = Mark(c_v, c_v.n)
            def fst(eng, tt=tt, h2=h2):
                return eng.dma_start(
                    d_out[:, :].rearrange("(q p) d -> q p d", p=128)[4 * h2 + tt],
                    BIG_xf4[tt])
            st_eng, st_ch = ("sync", c_st)
            P.emit(st_eng, fst, waits=mk_ob.waits(), inc=st_ch, inc_amt=16,
                   tag=f"h{h2}fs{tt}")
            st_m.append(Mark(st_ch, st_ch.n))
        acc_free = Mark(c_v, c_v.n)
        big_free = Mark(c_st, c_st.n)
        big_free2 = Mark(c_st2, c_st2.n)
        big_free_b = st_m[1]
        big_free_b2 = st_m[0]
        xn_free = qo_free
        xn_free = Mark(c_t, c_t.n)

    P.emit("sync", lambda eng: None,
           waits=[(c_st, c_st.n), (c_st2, c_st2.n)], tag="final")

    P.audit()
    with nc.Block() as block:
        P.lower(block)
    stack.close()
    return nc


_NC_CACHE = None


def _get_nc():
    global _NC_CACHE
    if _NC_CACHE is None:
        _NC_CACHE = build_nc()
    return _NC_CACHE


def kernel(L0_attn_x, L0_attn_Wq, L0_attn_bq, L0_attn_Wk, L0_attn_bk,
           L0_attn_Wv, L0_attn_bv, L0_attn_Wo, L0_attn_bo, L0_ffn_G,
           L0_ffn_gate_w, L0_ffn_up_w, L0_ffn_down_w, L0_ffn_lora_A,
           L0_ffn_lora_B):
    x = np.ascontiguousarray(np.asarray(L0_attn_x, dtype=np.float32))
    ident = np.eye(128, dtype=np.float32)
    esel16 = np.zeros((8, 128), dtype=np.float32)
    for e in range(8):
        esel16[e, 16 * e:16 * (e + 1)] = 1.0
    ones1 = np.ones((33, 128), dtype=np.float32)
    bvrep = np.ascontiguousarray(
        np.broadcast_to(np.asarray(L0_attn_bv, np.float32), (128, 256)))
    bqbo = np.stack([np.asarray(L0_attn_bq, np.float32),
                     np.asarray(L0_attn_bo, np.float32)])
    com = {
        "wq": np.ascontiguousarray(L0_attn_Wq, np.float32),
        "wk": np.ascontiguousarray(L0_attn_Wk, np.float32),
        "wv": np.ascontiguousarray(L0_attn_Wv, np.float32),
        "wo": np.ascontiguousarray(L0_attn_Wo, np.float32),
        "bqbo": np.ascontiguousarray(bqbo),
        "bk": np.ascontiguousarray(np.asarray(L0_attn_bk, np.float32).reshape(2, 128).T),
        "bvrep": bvrep,
        "gG": np.ascontiguousarray(L0_ffn_G, np.float32),
        "gate_w": np.ascontiguousarray(
            np.transpose(np.asarray(L0_ffn_gate_w, np.float32), (0, 2, 1))),
        "up_w": np.ascontiguousarray(
            np.transpose(np.asarray(L0_ffn_up_w, np.float32), (0, 2, 1))),
        "down_w": np.ascontiguousarray(
            np.transpose(np.asarray(L0_ffn_down_w, np.float32), (0, 2, 1))),
        "lora_A": np.ascontiguousarray(L0_ffn_lora_A, np.float32),
        "lora_B": np.ascontiguousarray(L0_ffn_lora_B, np.float32),
        "ident": ident,
        "epsv": np.full((128, 1), EPS_HOST, np.float32),
        "esel16": esel16,
        "ones1": ones1,
    }
    in_maps = []
    for core in range(8):
        b, p = divmod(core, 2)
        xb = x[b]
        xown = np.ascontiguousarray(xb.reshape(16, 128, D)[p::2].reshape(OWN, D))
        r = 128 * p
        jj = np.arange(256)[None, :]
        pp = np.arange(128)[:, None]
        dmask = np.where(jj <= pp + r, 0.0, -1e30).astype(np.float32)
        m = dict(com)
        m["xseq"] = xb
        m["xown"] = xown
        m["dmask"] = np.ascontiguousarray(dmask)
        in_maps.append(m)

    nc = _get_nc()
    res = run_bass_kernel_spmd(nc, in_maps, core_ids=list(range(8)))
    out = np.empty((B, T, D), dtype=np.float32)
    for core in range(8):
        b, p = divmod(core, 2)
        o = np.asarray(res.results[core]["out"]).reshape(8, 128, D)
        out.reshape(B, 16, 128, D)[b, p::2] = o
    return out


# version 30
# speedup vs baseline: 1.0066x; 1.0066x over previous
import math
import os
import numpy as np
import concourse.bass as bass
import concourse.mybir as mybir
from concourse.bass_utils import run_bass_kernel_spmd

F32 = mybir.dt.float32
F16 = mybir.dt.float16
AX = mybir.AxisListType.X
OP = mybir.AluOpType
ACT = mybir.ActivationFunctionType

ENGINES = ("sync", "gpsimd", "tensor", "vector", "scalar")

D = 2048
T = 2048
B = 4
NH = 16
NKV = 2
DH = 128
E = 8
FD = 256
R = 16
OWN = 1024
HALF = 512
ALPHA = 1.0 / math.sqrt(DH)
EPS = 1e-5
EPS_HOST = 1e-5


class Ch:
    def __init__(self, handle, name):
        self.h = handle
        self.name = name
        self.n = 0

    def bump(self, amt):
        self.n += amt
        return self.n


class Mark:
    __slots__ = ("ch", "val")

    def __init__(self, ch=None, val=0):
        self.ch = ch
        self.val = val

    def waits(self):
        return [(self.ch, self.val)] if self.ch is not None else []


class Prog:
    def __init__(self, chans):
        self.q = {e: [] for e in ENGINES}
        self.chans = chans

    def emit(self, eng, fn, waits=(), inc=None, inc_amt=1, tag=""):
        ws = [(ch, val) for ch, val in waits if ch is not None]
        if inc is not None:
            inc.bump(inc_amt)
        self.q[eng].append((tuple(ws), fn, inc, inc_amt, tag))

    def audit(self):
        heads = {e: 0 for e in ENGINES}
        cnt = {id(c): 0 for c in self.chans}
        progress = True
        while progress:
            progress = False
            for e in ENGINES:
                while heads[e] < len(self.q[e]):
                    ws, _, inc, amt, tag = self.q[e][heads[e]]
                    if any(cnt[id(ch)] < val for ch, val in ws):
                        break
                    if inc is not None:
                        cnt[id(inc)] += amt
                    heads[e] += 1
                    progress = True
        stuck = {e: heads[e] for e in ENGINES if heads[e] < len(self.q[e])}
        if stuck:
            lines = []
            for e, h in stuck.items():
                ws, _, _, _, tag = self.q[e][h]
                missing = [(ch.name, val, cnt[id(ch)]) for ch, val in ws
                           if cnt[id(ch)] < val]
                lines.append(f"  {e}@{h}/{len(self.q[e])} tag={tag} missing={missing}")
            raise RuntimeError("schedule deadlock:\n" + "\n".join(lines))

    def lower(self, block):
        def mk(ename):
            def body(eng):
                for ws, fn, inc, amt, tag in self.q[ename]:
                    for ch, val in ws:
                        eng.wait_ge(ch.h, val)
                    inst = fn(eng)
                    if inc is not None:
                        assert inst is not None, f"op with inc returned None: {tag}"
                        inst.then_inc(inc.h, amt)
            return body
        block.sync(mk("sync"))
        block.gpsimd(mk("gpsimd"))
        block.tensor(mk("tensor"))
        block.vector(mk("vector"))
        block.scalar(mk("scalar"))


def build_nc():
    nc = bass.Bass("TRN2", target_bir_lowering=False,
                   detect_race_conditions=False)

    d_xseq = nc.dram_tensor("xseq", [T, D], F32, kind="ExternalInput").ap()
    d_xown = nc.dram_tensor("xown", [OWN, D], F32, kind="ExternalInput").ap()
    d_wq = nc.dram_tensor("wq", [D, D], F32, kind="ExternalInput").ap()
    d_wk = nc.dram_tensor("wk", [D, 256], F32, kind="ExternalInput").ap()
    d_wv = nc.dram_tensor("wv", [D, 256], F32, kind="ExternalInput").ap()
    d_wo = nc.dram_tensor("wo", [D, D], F32, kind="ExternalInput").ap()
    d_bqbo = nc.dram_tensor("bqbo", [2, D], F32, kind="ExternalInput").ap()
    d_bk = nc.dram_tensor("bk", [128, 2], F32, kind="ExternalInput").ap()
    d_bvrep = nc.dram_tensor("bvrep", [128, 256], F32, kind="ExternalInput").ap()
    d_G = nc.dram_tensor("gG", [D, E], F32, kind="ExternalInput").ap()
    d_gate = nc.dram_tensor("gate_w", [E, D, FD], F32, kind="ExternalInput").ap()
    d_up = nc.dram_tensor("up_w", [E, D, FD], F32, kind="ExternalInput").ap()
    d_down = nc.dram_tensor("down_w", [E, FD, D], F32, kind="ExternalInput").ap()
    d_lA = nc.dram_tensor("lora_A", [E, R, D], F32, kind="ExternalInput").ap()
    d_lB = nc.dram_tensor("lora_B", [E, D, R], F32, kind="ExternalInput").ap()
    d_ident = nc.dram_tensor("ident", [128, 128], F32, kind="ExternalInput").ap()
    d_mask = nc.dram_tensor("dmask", [128, 256], F32, kind="ExternalInput").ap()
    d_esel16 = nc.dram_tensor("esel16", [8, 128], F32, kind="ExternalInput").ap()
    d_ones = nc.dram_tensor("ones1", [33, 128], F32, kind="ExternalInput").ap()
    d_eps = nc.dram_tensor("epsv", [128, 1], F32, kind="ExternalInput").ap()
    d_out = nc.dram_tensor("out", [OWN, D], F32, kind="ExternalOutput").ap()
    DBG = bool(os.environ.get("KDBG"))
    if DBG:
        d_dbgO = [nc.dram_tensor(f"dbgO{i}", [128, 8192], F32,
                                 kind="ExternalOutput").ap() for i in range(2)]
        d_dbgP = nc.dram_tensor("dbgP", [128, 4096 + 2048], F32,
                                kind="ExternalOutput").ap()
        d_dbgQ = [nc.dram_tensor(f"dbgQ{i}", [128, 8192], F32,
                                 kind="ExternalOutput").ap() for i in range(2)]
        d_dbgK = nc.dram_tensor("dbgK", [128, 4096], F32, kind="ExternalOutput").ap()
        d_dbgV = nc.dram_tensor("dbgV", [128, 4096], F32, kind="ExternalOutput").ap()
        d_dbgXN = nc.dram_tensor("dbgXN", [128, 8192], F32, kind="ExternalOutput").ap()
        d_dbgLN = nc.dram_tensor("dbgLN", [128, 12288], F32, kind="ExternalOutput").ap()
        d_dbgST = nc.dram_tensor("dbgST", [128, 4096], F32, kind="ExternalOutput").ap()

    sb = nc.alloc_sbuf_tensor
    s_K = sb("s_K", [128, 2, T], F32)
    s_V = sb("s_V", [128, 16, 256], F32)
    s_XN = sb("s_XN", [128, 8192], F32)
    s_QO = sb("s_QO", [128, 8192], F32)
    s_ACC = sb("s_ACC", [128, 8192], F32)
    s_BIG = sb("s_BIG", [128, 14336], F32)
    s_SM = sb("s_SM", [128, 4096], F32)
    s_ident = sb("s_ident", [128, 128], F32)
    s_mask = sb("s_mask", [128, 256], F32)
    s_esel16 = sb("s_esel16", [8, 128], F32)
    s_bk = sb("s_bk", [128, 2], F32)
    s_bvrep = sb("s_bvrep", [128, 256], F32)
    s_G = sb("s_G", [128, 16, 8], F32)
    s_Gp = sb("s_Gp", [128, 128], F32)
    s_ones = sb("s_ones", [33, 128], F32)
    s_eps = sb("s_eps", [128, 1], F32)

    def v(arena, lo, sz, shape=None):
        ap = arena[:, lo:lo + sz]
        if shape is not None:
            ap = ap.rearrange("p (a b) -> p a b", a=shape[0], b=shape[1])
        return ap

    QO_Q = v(s_QO, 0, 8192, (16, HALF))
    QO_gated = v(s_QO, 0, 1024, (2, HALF))
    QO_gu = v(s_QO, 1024, 1024, (2, HALF))
    QO_lAT = v(s_QO, 4096, 2048, (16, 128))
    QO_lBT = v(s_QO, 6144, 2048, (16, 128))

    ACC_acc = v(s_ACC, 0, 8192, (4, D))
    ACC_wkv = v(s_ACC, 0, 8192, (16, 512))

    BIG_xtA = v(s_BIG, 0, 4096, (2, D))
    BIG_xtB = v(s_BIG, 4096, 4096, (2, D))
    BIG_xnt = v(s_BIG, 8192, 4096, (2, D))
    BIG_xntB = v(s_QO, 4096, 4096, (2, D))
    BIG_xnf = v(s_QO, 0, 4096, (16, 256))
    BIG_probs = [v(s_BIG, 0, 2048, None), v(s_BIG, 2048, 2048, None)]
    BIG_probsT = v(s_BIG, 4096, 2048, None)
    BIG_wcb = [v(s_BIG, 4096, 4096, (16, 256)),
               v(s_BIG, 8192, 4096, (16, 256))]
    BIG_qstage = [v(s_BIG, 12288, 512, None), v(s_BIG, 12800, 512, None)]
    BIG_wnat = v(s_BIG, 0, 4096, (2, D))
    BIG_wnat2 = v(s_BIG, 0, 4096, (16, 256))
    BIG_dnat = v(s_BIG, 4096, 2048, (8, 256))
    BIG_dnat2 = v(s_BIG, 4096, 2048, None)
    BIG_wT = v(s_BIG, 6144, 4096, (16, 256))
    BIG_dT = v(s_BIG, 10240, 4096, (2, D))
    BIG_lstA = [s_BIG[0:16, 6144:8192],
                s_BIG[0:16, 8192:10240]]
    BIG_stBn = [v(s_BIG, 10240, 256, (16, 16)),
                v(s_BIG, 10496, 256, (16, 16))]
    BIG_stBr = s_BIG[0:16, 10752:12800]
    BIG_obuf = [v(s_BIG, 0, 2048, None), v(s_BIG, 2048, 2048, None)]
    BIG_xfin = [v(s_BIG, 4096, 2048, None), v(s_BIG, 6144, 2048, None)]
    BIG_xf4 = [v(s_BIG, 4096, 2048, None), v(s_BIG, 6144, 2048, None),
               v(s_BIG, 0, 2048, None), v(s_BIG, 2048, 2048, None)]

    SM_ell = s_SM[:, 0:16]
    SM_ei = s_SM[:, 16:32]
    SM_gstat = v(s_SM, 32, 64, (4, 16))
    SM_gsc = v(s_SM, 96, 32, (4, 8))
    SM_gst = v(s_SM, 128, 32, (4, 8))
    SM_stats = v(s_SM, 160, 32, (4, 8))
    SM_sgst = s_SM[0:8, 192:704]
    SM_lr = s_SM[:, 704:1216]
    SM_wfm = s_SM[0:8, 1216:1728]
    SM_browq = s_SM[0:1, 2048:4096]
    SM_browo = s_SM[32:33, 2048:4096]

    def vh(arena, lo, sz, shape=None, p0=0, p1=128):
        ap = arena[p0:p1, lo:lo + sz].bitcast(F16)
        if shape is not None:
            ap = ap.rearrange("p (a b) -> p a b", a=shape[0], b=shape[1])
        return ap

    XN_h = vh(s_XN, 0, 4096, (16, HALF))
    XN_l = vh(s_XN, 4096, 4096, (16, HALF))
    G_h = vh(s_Gp, 0, 64, (16, 8))
    G_l = vh(s_Gp, 64, 64, (16, 8))
    Q_wst = v(s_BIG, 0, 4096, (16, 256))
    Q_wst2 = v(s_ACC, 0, 4096, (16, 256))
    Q_wh = [vh(s_BIG, 4096, 2048, (16, 256)), vh(s_BIG, 8192, 2048, (16, 256))]
    Q_wl = [vh(s_BIG, 6144, 2048, (16, 256)), vh(s_BIG, 10240, 2048, (16, 256))]
    WT_h = vh(s_BIG, 6144, 2048, (16, 256))
    WT_l = vh(s_BIG, 8192, 2048, (16, 256))
    DT_h = vh(s_BIG, 10240, 2048, (2, D))
    DT_l = vh(s_BIG, 12288, 2048, (2, D))
    QO_gt = v(s_QO, 0, 1024, (2, HALF))
    QO_gh = vh(s_QO, 1024, 512, (2, HALF))
    QO_gl = vh(s_QO, 1536, 512, (2, HALF))
    QO_guf = v(s_QO, 2048, 1024, (2, HALF))
    LAT_h = vh(s_QO, 4096, 1024, (16, 128))
    LAT_l = vh(s_QO, 5120, 1024, (16, 128))
    LBT_h = vh(s_QO, 6144, 1024, (16, 128))
    LBT_l = vh(s_QO, 7168, 1024, (16, 128))
    SBR_h = vh(s_BIG, 10752, 1024, None, 0, 16)
    SBR_l = vh(s_BIG, 11776, 1024, None, 0, 16)
    LR_h = vh(s_SM, 192, 256, None)
    LR_l = vh(s_SM, 448, 256, None)

    ps = [nc.alloc_psum_tensor(f"pb{i}", [128, 512], F32) for i in range(8)]

    import contextlib
    stack = contextlib.ExitStack()
    chans = []

    def mkch(name):
        ch = Ch(stack.enter_context(nc.semaphore(name)), name)
        chans.append(ch)
        return ch

    c_dx = mkch("dx")
    c_dw = mkch("dw")
    c_dg = mkch("dg")
    c_st = mkch("st")
    c_t = mkch("t")
    c_v = mkch("v")
    c_s = mkch("s")

    P = Prog(chans)
    bw = [Mark() for _ in range(8)]
    br = [Mark() for _ in range(8)]

    def dma(eng_name, ch, dst, src, waits=(), tag=""):
        def fn(eng, dst=dst, src=src):
            return eng.dma_start(dst, src)
        P.emit(eng_name, fn, waits=waits, inc=ch, inc_amt=16, tag=tag)
        return Mark(ch, ch.n)

    def mm(bank_i, triples, waits=(), tag=""):
        ws = list(br[bank_i].waits()) + list(waits)
        def fn(eng, triples=triples):
            inst = None
            for lhsT, rhs, out, st_f, sp_f in triples:
                inst = eng.matmul(out, lhsT, rhs, start=st_f, stop=sp_f)
            return inst
        P.emit("tensor", fn, waits=ws, inc=c_t, tag=tag)
        bw[bank_i] = Mark(c_t, c_t.n)
        return bw[bank_i]

    def trg(bank_i, items, waits=(), tag=""):
        ws = list(br[bank_i].waits()) + list(waits)
        def fn(eng, items=items):
            inst = None
            for out, in_, idn in items:
                inst = eng.transpose(out, in_, idn)
            return inst
        P.emit("tensor", fn, waits=ws, inc=c_t, tag=tag)
        bw[bank_i] = Mark(c_t, c_t.n)
        return bw[bank_i]

    def evac(bank_i, eng_name, fn, waits=(), tag=""):
        ch = c_v if eng_name == "vector" else c_s
        ws = list(bw[bank_i].waits()) + list(waits)
        P.emit(eng_name, fn, waits=ws, inc=ch, tag=tag)
        br[bank_i] = Mark(ch, ch.n)
        return br[bank_i]

    dma("sync", c_dx, s_ident[:, :], d_ident, tag="ident")
    dma("sync", c_dx, s_mask[:, :], d_mask, tag="mask")
    dma("sync", c_dx, s_esel16[:, :], d_esel16, tag="esel16")
    dma("sync", c_dx, s_bk[:, :], d_bk, tag="bk")
    dma("sync", c_dx, s_bvrep[:, :], d_bvrep, tag="bvrep")
    dma("sync", c_dx, s_G[:, :, :],
        d_G.rearrange("(k p) e -> p k e", p=128), tag="G")
    dma("sync", c_dx, s_ones[:, :], d_ones, tag="ones")
    dma("sync", c_dx, s_eps[:, :], d_eps, tag="eps")
    dma("sync", c_dx, SM_browq, d_bqbo[0:1, :], tag="browq")
    dma("sync", c_dx, SM_browo, d_bqbo[1:2, :], tag="browo")
    m_consts = Mark(c_dx, c_dx.n)
    P.emit("scalar", lambda eng: eng.copy(G_h[:, :, :], s_G[:, :, :]),
           waits=m_consts.waits(), inc=c_s, tag="Gh")
    mk_Gh = Mark(c_s, c_s.n)
    P.emit("vector", lambda eng: eng.tensor_sub(G_l[:, :, :], s_G[:, :, :],
                                                G_h[:, :, :]),
           waits=mk_Gh.waits(), inc=c_v, tag="Gl")
    mk_Gp = Mark(c_v, c_v.n)
    m_wkv = dma("sync", c_dw, ACC_wkv[:, :, 0:256],
                d_wk.rearrange("(k p) d -> p k d", p=128), tag="wk")
    m_wkv = dma("sync", c_dw, ACC_wkv[:, :, 256:512],
                d_wv.rearrange("(k p) d -> p k d", p=128), tag="wv")

    def emit_ln(xt_view, xnt_view, nchunk, dep, tag, guard=None, st_off=0):
        if guard is None:
            gw = []
        elif isinstance(guard, list):
            gw = [w for g in guard for w in g.waits()]
        else:
            gw = guard.waits()
        for cI in range(nchunk):
            xi = xt_view[:, cI, :]
            st = SM_stats[:, st_off + cI, :]
            P.emit("scalar",
                   lambda eng, xi=xi, st=st, cI=cI: eng.activation(
                       xnt_view[:, cI, :], xi, ACT.Square, accum_out=st[:, 1:2]),
                   waits=list(dep.waits()) + (gw if cI == 0 else []),
                   inc=c_s, tag=f"{tag}sq{cI}")
            mk_sq = Mark(c_s, c_s.n)
            P.emit("vector",
                   lambda eng, xi=xi, st=st: eng.reduce_sum(st[:, 0:1], xi, axis=AX),
                   waits=dep.waits(), inc=c_v, tag=f"{tag}sum{cI}")
            mk_a = Mark(c_v, c_v.n)
            P.emit("vector",
                   lambda eng, st=st: eng.tensor_scalar_mul(
                       st[:, 2:4], st[:, 0:2], 1.0 / D),
                   waits=[(c_s, mk_sq.val), (c_v, mk_a.val)],
                   inc=c_v, tag=f"{tag}m{cI}")
            mk_b = Mark(c_v, c_v.n)
            P.emit("vector",
                   lambda eng, st=st: eng.tensor_mul(st[:, 4:5], st[:, 2:3], st[:, 2:3]),
                   waits=[(c_v, mk_b.val)], inc=c_v, tag=f"{tag}mm{cI}")
            mk_c = Mark(c_v, c_v.n)
            P.emit("vector",
                   lambda eng, st=st: eng.tensor_sub(st[:, 4:5], st[:, 3:4], st[:, 4:5]),
                   waits=[(c_v, mk_c.val)], inc=c_v, tag=f"{tag}var{cI}")
            mk_var = Mark(c_v, c_v.n)
            P.emit("scalar",
                   lambda eng, st=st: eng.activation(st[:, 5:6], st[:, 4:5],
                                                     ACT.Ln, bias=s_eps[:, :]),
                   waits=[(c_v, mk_var.val)], inc=c_s, tag=f"{tag}ln{cI}")
            mk_ln2 = Mark(c_s, c_s.n)
            P.emit("scalar",
                   lambda eng, st=st: eng.activation(st[:, 6:7], st[:, 5:6],
                                                     ACT.Exp, scale=-0.5),
                   waits=[(c_s, mk_ln2.val)], inc=c_s, tag=f"{tag}rstd{cI}")
            mk_rstd = Mark(c_s, c_s.n)
            P.emit("vector",
                   lambda eng, xi=xi, st=st, cI=cI: eng.tensor_scalar(
                       xnt_view[:, cI, :], xi, st[:, 2:3], st[:, 6:7],
                       OP.subtract, OP.mult),
                   waits=[(c_s, mk_rstd.val), (c_v, mk_var.val)],
                   inc=c_v, tag=f"{tag}xn{cI}")
        return Mark(c_v, c_v.n)

    xt_free = [Mark(), Mark()]
    xnt_guard = Mark()
    xts = [BIG_xtA, BIG_xtB]
    xnts = [BIG_xnt, BIG_xntB]
    src0 = d_xseq.rearrange("(s c p) d -> s p c d", p=128, c=2)[0]
    m_x0 = dma("sync", c_dx, xts[0][:, :, :], src0, waits=[], tag="ab1x0")
    mk_lns = {0: emit_ln(xts[0], xnts[0], 2, m_x0, "a1l0_", guard=None)}
    xt_free[0] = mk_lns[0]
    for sti in range(8):
        buf = sti % 2
        xnt_cur = xnts[buf]
        mk_ln = mk_lns[sti]
        if DBG and sti == 1:
            mt1 = dma("sync", c_dx, d_dbgLN, s_BIG[:, 0:12288],
                      waits=mk_ln.waits(), tag="dbgLN")
            mt2 = dma("sync", c_dx, d_dbgST, s_SM[:, :], waits=mk_ln.waits(),
                      tag="dbgST")
            xt_free[buf] = Mark(c_dx, mt2.val)
        mk_fm = Mark()
        for i0 in range(0, 32, 4):
            cc, k0 = divmod(i0, 16)
            bank = (4, 5, 7)[(i0 // 4) % 3]
            items = [(ps[bank][:, 128 * jj:128 * (jj + 1)],
                      xnt_cur[:, cc, 128 * (k0 + jj):128 * (k0 + jj + 1)],
                      s_ident[:, :]) for jj in range(4)]
            trg(bank, items, waits=mk_ln.waits(), tag=f"a1T{sti}_{i0}")
            def ev(eng, cc=cc, k0=k0, bank=bank):
                return eng.tensor_copy(
                    BIG_xnf[:, k0:k0 + 4, 128 * cc:128 * (cc + 1)],
                    ps[bank][:, 0:512])
            evac(bank, "vector", ev, tag=f"a1Tev{sti}_{i0}")
            mk_fm = Mark(c_v, c_v.n)
        xnt_guard = Mark(c_t, c_t.n)
        if sti + 1 < 8:
            nbuf = (sti + 1) % 2
            srcn = d_xseq.rearrange("(s c p) d -> s p c d", p=128, c=2)[sti + 1]
            m_xn2 = dma("sync", c_dx, xts[nbuf][:, :, :], srcn,
                        waits=xt_free[nbuf].waits(), tag=f"ab1x{sti + 1}")
            mk_lns[sti + 1] = emit_ln(xts[nbuf], xnts[nbuf], 2, m_xn2,
                                      f"a1l{sti + 1}_", guard=xnt_guard)
            xt_free[nbuf] = mk_lns[sti + 1]
        else:
            src0b = d_xown.rearrange("(s c p) d -> s p c d", p=128, c=2)[0]
            m_ab2x0 = dma("sync", c_dx, xts[0][:, :, :], src0b,
                          waits=xt_free[0].waits(), tag="h0x0")
            mk_ab2ln0 = emit_ln(xts[0], BIG_xnt, 2, m_ab2x0, "h0l0_",
                                guard=xnt_guard, st_off=2)
            xt_free[0] = mk_ab2ln0
        for dcI in range(2):
            bank = dcI
            triples = [(ACC_wkv[:, k, 128 * dcI:128 * (dcI + 1)],
                        BIG_xnf[:, k, :].rearrange("p a b -> p (a b)") if False
                        else BIG_xnf[:, k, 0:256],
                        ps[bank][:, 0:256], k == 0, k == 15) for k in range(16)]
            mm(bank, triples,
               waits=list(mk_fm.waits()) + list(m_wkv.waits()), tag=f"a1K{sti}_{dcI}")
            def evK(eng, bank=bank, dcI=dcI, sti=sti):
                return eng.activation(s_K[:, dcI, 256 * sti:256 * (sti + 1)],
                                      ps[bank][:, 0:256], ACT.Identity,
                                      bias=s_bk[:, dcI:dcI + 1])
            evac(bank, "scalar", evK, waits=m_consts.waits(), tag=f"a1Ke{sti}_{dcI}")
        for ss in range(2):
            bank = 2 + ss
            triples = [(BIG_xnf[:, k, 128 * ss:128 * (ss + 1)],
                        ACC_wkv[:, k, 256:512],
                        ps[bank][:, 0:256], k == 0, k == 15) for k in range(16)]
            mm(bank, triples, waits=list(mk_fm.waits()) + list(m_wkv.waits()),
               tag=f"a1V{sti}_{ss}")
            def evV(eng, bank=bank, sti=sti, ss=ss):
                return eng.tensor_add(s_V[:, 2 * sti + ss, :],
                                      ps[bank][:, 0:256], s_bvrep[:, :])
            evac(bank, "vector", evV, waits=m_consts.waits(), tag=f"a1Ve{sti}_{ss}")
    mark_K = Mark(c_s, c_s.n)
    mark_V = Mark(c_v, c_v.n)
    if DBG:
        mk_ = dma("sync", c_dx, d_dbgK, s_K[:, :, :].rearrange("p a b -> p (a b)"),
                  waits=mark_K.waits(), tag="dbgK")
        mark_K = Mark(c_dx, mk_.val)
        mv_ = dma("sync", c_dx, d_dbgV, s_V[:, :, :].rearrange("p a b -> p (a b)"),
                  waits=mark_V.waits(), tag="dbgV")
        mark_V = Mark(c_dx, mv_.val)
    mark_ab1_t = Mark(c_t, c_t.n)

    acc_free = Mark()
    acc_free = mark_ab1_t
    qo_free = mark_ab1_t
    xn_free = Mark()
    big_free = Mark(c_v, c_v.n)

    pvq_r = [Mark() for _ in range(4)]

    store_total = 0

    for h2 in range(2):
        mk_fm = Mark()
        for st2 in range(2):
            buf = (st2 + h2) % 2
            src = d_xown.rearrange("(s c p) d -> s p c d", p=128, c=2)[2 * h2 + st2]
            if h2 == 0:
                extra = list(big_free.waits()) if st2 == 0 else []
            else:
                extra = list(big_free_b.waits()) if buf == 1 else list(big_free.waits())
            xnt_cur = BIG_xnt if buf == 0 else BIG_xntB
            if h2 == 0 and st2 == 0:
                mk_ln = mk_ab2ln0
            else:
                m_x = dma("sync", c_dx, xts[buf][:, :, :], src,
                          waits=list(xt_free[buf].waits()) + extra,
                          tag=f"h{h2}x{st2}")
                mk_ln = emit_ln(xts[buf], xnt_cur, 2, m_x, f"h{h2}l{st2}_",
                                guard=[xnt_guard, qo_free])
            xt_free[buf] = mk_ln
            for i0 in range(0, 32, 4):
                cc, k0 = divmod(i0, 16)
                bank = (4, 5, 7)[(i0 // 4) % 3]
                items = [(ps[bank][:, 128 * jj:128 * (jj + 1)],
                          xnt_cur[:, cc, 128 * (k0 + jj):128 * (k0 + jj + 1)],
                          s_ident[:, :]) for jj in range(4)]
                trg(bank, items,
                    waits=list(mk_ln.waits()) + (list(xn_free.waits()) if (st2 == 0 and i0 == 0) else []),
                    tag=f"h{h2}T{st2}_{i0}")
                sl = slice(256 * st2 + 128 * cc, 256 * st2 + 128 * (cc + 1))
                def evh(eng, k0=k0, bank=bank, sl=sl):
                    return eng.copy(XN_h[:, k0:k0 + 4, sl], ps[bank][:, 0:512])
                P.emit("scalar", evh, waits=bw[bank].waits(), inc=c_s,
                       tag=f"h{h2}Th{st2}_{i0}")
                mk_h = Mark(c_s, c_s.n)
                def evl(eng, k0=k0, bank=bank, sl=sl):
                    return eng.tensor_sub(XN_l[:, k0:k0 + 4, sl],
                                          ps[bank][:, 0:512],
                                          XN_h[:, k0:k0 + 4, sl])
                evac(bank, "vector", evl, waits=mk_h.waits(),
                     tag=f"h{h2}Te{st2}_{i0}")
                mk_fm = Mark(c_v, c_v.n)
            xnt_guard = Mark(c_t, c_t.n)
        mark_xn = mk_fm
        if DBG and h2 == 0:
            mxn = dma("sync", c_dx, d_dbgXN,
                      s_XN[:, :, :].rearrange("p a b -> p (a b)"),
                      waits=mark_xn.waits(), tag="dbgXN")
            mark_xn = Mark(c_dx, mxn.val)
        mark_ab2_t = Mark(c_t, c_t.n)

        wsts = [Q_wst, Q_wst2]
        wst_free2 = [xt_free[0], acc_free]
        pair_free = [xt_free[1], mark_ab2_t]
        qst_free = [Mark(), Mark()]

        def q_split(fb):
            wbuf = fb % 2
            stg_w = wsts[wbuf]
            src = d_wq.rearrange("(k p) d -> p k d", p=128)[:, :, 256 * fb:256 * (fb + 1)]
            m_w = dma("sync", c_dw, stg_w[:, :, :], src,
                      waits=list(wst_free2[wbuf].waits()),
                      tag=f"h{h2}qw{fb}")
            P.emit("scalar",
                   lambda eng, wbuf=wbuf, stg_w=stg_w: eng.copy(
                       Q_wh[wbuf][:, :, :], stg_w[:, :, :]),
                   waits=list(m_w.waits()) + list(pair_free[wbuf].waits()),
                   inc=c_s, tag=f"h{h2}qsh{fb}")
            mk_wh = Mark(c_s, c_s.n)
            P.emit("vector",
                   lambda eng, wbuf=wbuf, stg_w=stg_w: eng.tensor_sub(
                       Q_wl[wbuf][:, :, :], stg_w[:, :, :], Q_wh[wbuf][:, :, :]),
                   waits=mk_wh.waits(), inc=c_v, tag=f"h{h2}qsl{fb}")
            mk_wp = Mark(c_v, c_v.n)
            wst_free2[wbuf] = mk_wp
            return mk_wp

        mk_wps = {0: q_split(0)}
        for fb in range(8):
            wbuf = fb % 2
            mk_wp = mk_wps[fb]
            if fb + 1 < 8:
                mk_wps[fb + 1] = q_split(fb + 1)
            for tt in range(4):
                bank = tt
                triples = []
                for k in range(16):
                    xh = XN_h[:, k, 128 * tt:128 * (tt + 1)]
                    xl = XN_l[:, k, 128 * tt:128 * (tt + 1)]
                    triples.append((xh, Q_wh[wbuf][:, k, :], ps[bank][:, 0:256],
                                    k == 0, False))
                    triples.append((xh, Q_wl[wbuf][:, k, :], ps[bank][:, 0:256],
                                    False, False))
                    triples.append((xl, Q_wh[wbuf][:, k, :], ps[bank][:, 0:256],
                                    False, False))
                triples.append((s_ones[0:1, :], SM_browq[:, 256 * fb:256 * (fb + 1)],
                                ps[bank][:, 0:256], False, True))
                mm(bank, triples,
                   waits=list(mk_wp.waits()) + (list(mark_xn.waits()) if (fb == 0) else []) +
                         (list(m_consts.waits()) if (h2 == 0 and fb == 0) else []),
                   tag=f"h{h2}qm{fb}_{tt}")
                pair_free[wbuf] = Mark(c_t, c_t.n)
                sbuf_stage = BIG_qstage[tt % 2]
                def evq(eng, bank=bank, stg=sbuf_stage):
                    return eng.tensor_copy(stg[:, 0:256], ps[bank][:, 0:256])
                evac(bank, "vector", evq,
                     waits=qst_free[tt % 2].waits(),
                     tag=f"h{h2}qe{fb}_{tt}")
                mk_stage = Mark(c_v, c_v.n)
                tbank = 4 + tt % 2
                items = [(ps[tbank][:, 128 * jj:128 * (jj + 1)],
                          sbuf_stage[:, 128 * jj:128 * (jj + 1)], s_ident[:, :])
                         for jj in range(2)]
                trg(tbank, items, waits=mk_stage.waits(), tag=f"h{h2}qT{fb}_{tt}")
                qst_free[tt % 2] = Mark(c_t, c_t.n)
                def evqs(eng, tbank=tbank, fb=fb, tt=tt):
                    return eng.activation(
                        QO_Q[:, 2 * fb:2 * fb + 2, 128 * tt:128 * (tt + 1)],
                        ps[tbank][:, 0:256], ACT.Copy, scale=ALPHA)
                evac(tbank, "scalar", evqs,
                     waits=(qo_free.waits() if (fb == 0 and tt == 0) else []),
                     tag=f"h{h2}qs{fb}_{tt}")
        wcb_free = [pair_free[0], pair_free[1]]
        wst2_free = wst_free2[1]
        mark_Q = Mark(c_s, c_s.n)
        if DBG:
            mq = dma("sync", c_dx, d_dbgQ[h2], s_QO[:, :],
                     waits=mark_Q.waits(), tag=f"dbgQ{h2}")
            mark_Q = Mark(c_dx, mq.val)

        heads = [(tt, h) for tt in range(4) for h in range(NH)]
        state = {}

        def at_scores(tt, h, first):
            j = 4 * h2 + tt
            c = j + 1
            nb = (c + 1) // 2
            ws = []
            for bk_i in range(nb):
                ws += br[bk_i].waits()
            if first:
                ws += list(mark_K.waits()) + list(mark_V.waits()) + list(mark_Q.waits())
            def fn(eng, tt=tt, h=h, c=c):
                inst = None
                lhsT = QO_Q[:, h, 128 * tt:128 * (tt + 1)]
                for i in range(c):
                    bank, halfc = divmod(i, 2)
                    out = ps[bank][:, 256 * halfc:256 * (halfc + 1)]
                    inst = eng.matmul(out, lhsT, s_K[:, h // 8, 256 * i:256 * (i + 1)],
                                      start=True, stop=True)
                return inst
            P.emit("tensor", fn, waits=ws, inc=c_t, tag=f"h{h2}sc{tt}_{h}")
            mk_sc = Mark(c_t, c_t.n)
            for bk_i in range(nb):
                bw[bk_i] = mk_sc
            return mk_sc

        def at_evmax(tt, h, mk_sc, first):
            j = 4 * h2 + tt
            c = j + 1
            nb = (c + 1) // 2
            pb_ = BIG_probs[h % 2]
            bank_d, half_d = divmod(c - 1, 2)
            def fev(eng, c=c, nb=nb, pb_=pb_, bank_d=bank_d, half_d=half_d):
                for bk_i in range(nb):
                    chunks = [i for i in (2 * bk_i, 2 * bk_i + 1)
                              if i < c and i != c - 1]
                    if len(chunks) == 2:
                        eng.tensor_copy(pb_[:, 512 * bk_i:512 * bk_i + 512],
                                        ps[bk_i][:, 0:512])
                    elif len(chunks) == 1:
                        i = chunks[0]
                        eng.tensor_copy(pb_[:, 256 * i:256 * (i + 1)],
                                        ps[bk_i][:, 256 * (i % 2):256 * (i % 2) + 256])
                return eng.tensor_add(pb_[:, 256 * (c - 1):256 * c],
                                      ps[bank_d][:, 256 * half_d:256 * (half_d + 1)],
                                      s_mask[:, :])
            ws_e = list(mk_sc.waits())
            if first:
                ws_e += list(m_consts.waits())
            prev2 = state.get("probs_free", {}).get(h % 2)
            if prev2 is not None:
                ws_e += prev2.waits()
            P.emit("vector", fev, waits=ws_e, inc=c_v, tag=f"h{h2}se{tt}_{h}")
            mk_ev = Mark(c_v, c_v.n)
            for bk_i in range(nb):
                br[bk_i] = mk_ev
            return mk_ev

        def at_max(tt, h, mk_ev):
            c = 4 * h2 + tt + 1
            pb_ = BIG_probs[h % 2]
            cm = 1 + (h % 2)
            P.emit("vector",
                   lambda eng, c=c, pb_=pb_, cm=cm: eng.reduce_max(
                       SM_ei[:, cm:cm + 1], pb_[:, 0:256 * c], axis=AX, negate=True),
                   waits=mk_ev.waits(), inc=c_v, tag=f"h{h2}mx{tt}_{h}")
            return Mark(c_v, c_v.n)

        def at_exp(tt, h, mk_max):
            c = 4 * h2 + tt + 1
            pb_ = BIG_probs[h % 2]
            cm = 1 + (h % 2)
            def fne(eng, c=c, pb_=pb_, h=h, cm=cm):
                return eng.activation(pb_[:, 0:256 * c], pb_[:, 0:256 * c],
                                      ACT.Exp, bias=SM_ei[:, cm:cm + 1],
                                      accum_out=SM_ell[:, h:h + 1])
            P.emit("scalar", fne, waits=mk_max.waits(), inc=c_s, tag=f"h{h2}ex{tt}_{h}")
            return Mark(c_s, c_s.n)

        def at_norm(tt, h, mk_exp):
            c = 4 * h2 + tt + 1
            pb_ = BIG_probs[h % 2]
            ci = 4 + (h % 2)
            P.emit("vector",
                   lambda eng, h=h, ci=ci: eng.reciprocal(SM_ei[:, ci:ci + 1],
                                                          SM_ell[:, h:h + 1]),
                   waits=mk_exp.waits(), inc=c_v, tag=f"h{h2}nm2{tt}_{h}")
            mk_n2 = Mark(c_v, c_v.n)
            P.emit("vector",
                   lambda eng, c=c, pb_=pb_, ci=ci: eng.tensor_scalar_mul(
                       pb_[:, 0:256 * c], pb_[:, 0:256 * c], SM_ei[:, ci:ci + 1]),
                   waits=mk_n2.waits(), inc=c_v, tag=f"h{h2}nm3{tt}_{h}")
            return Mark(c_v, c_v.n)

        def at_trg(tt, h, mk_norm, lo_g, hi_g):
            c = 4 * h2 + tt + 1
            pb_ = BIG_probs[h % 2]
            nsub = 2 * c
            grps = []
            for gi in range((nsub + 3) // 4)[lo_g:hi_g]:
                i0 = 4 * gi
                grp = list(range(i0, min(i0 + 4, nsub)))
                bank = (4, 5, 7)[gi % 3]
                items = [(ps[bank][:, 128 * jj:128 * (jj + 1)],
                          pb_[:, 128 * ii:128 * (ii + 1)], s_ident[:, :])
                         for jj, ii in enumerate(grp)]
                trg(bank, items, waits=mk_norm.waits() if i0 == 0 else [],
                    tag=f"h{h2}pT{tt}_{h}_{i0}")
                grps.append((grp, bank))
            return grps

        def at_evp(tt, h, grps, first_grp):
            mk_ev = Mark()
            prev_pv = state.get("pv_mark")
            for gi, (grp, bank) in enumerate(grps):
                ws_ev = (prev_pv.waits()
                         if (gi == 0 and first_grp and prev_pv is not None) else [])
                def evp(eng, grp=grp, bank=bank):
                    w = 128 * len(grp)
                    return eng.tensor_copy(BIG_probsT[:, 128 * grp[0]:128 * grp[0] + w],
                                           ps[bank][:, 0:w])
                evac(bank, "vector", evp, waits=ws_ev, tag=f"h{h2}pE{tt}_{h}_{grp[0]}")
                mk_ev = Mark(c_v, c_v.n)
            return mk_ev

        def at_trg_evp(tt, h, mk_norm):
            c = 4 * h2 + tt + 1
            ngrp = (2 * c + 3) // 4
            g_a = at_trg(tt, h, mk_norm, 0, min(3, ngrp))
            ev1 = at_evp(tt, h, g_a[0:1], first_grp=True)
            g_b = at_trg(tt, h, mk_norm, 3, ngrp)
            state.setdefault("probs_free", {})[h % 2] = Mark(c_t, c_t.n)
            ev2 = at_evp(tt, h, g_a[1:] + g_b, first_grp=False)
            return ev2 if (len(g_a) + len(g_b)) > 1 else ev1

        def at_pv(tt, h, mk_ev):
            c = 4 * h2 + tt + 1
            qr = h % 4
            def fpv(eng, c=c, h=h, qr=qr):
                inst = None
                out = ps[6][:, 128 * qr:128 * (qr + 1)]
                for i2 in range(2 * c):
                    inst = eng.matmul(out, s_V[:, i2, 128 * (h // 8):128 * (h // 8 + 1)],
                                      BIG_probsT[:, 128 * i2:128 * (i2 + 1)],
                                      start=(i2 == 0), stop=(i2 == 2 * c - 1))
                return inst
            P.emit("tensor", fpv,
                   waits=[(c_v, mk_ev.val)] + pvq_r[qr].waits(),
                   inc=c_t, tag=f"h{h2}pv{tt}_{h}")
            mk_pv = Mark(c_t, c_t.n)
            state["pv_mark"] = mk_pv
            def evo(eng, h=h, tt=tt, qr=qr):
                return eng.copy(QO_Q[:, h, 128 * tt:128 * (tt + 1)],
                                ps[6][:, 128 * qr:128 * (qr + 1)])
            P.emit("scalar", evo, waits=mk_pv.waits(), inc=c_s, tag=f"h{h2}oe{tt}_{h}")
            pvq_r[qr] = Mark(c_s, c_s.n)

        prev = None
        for idx, (tt, h) in enumerate(heads):
            if prev is not None:
                mk_norm = at_norm(prev[0], prev[1], prev[2])
            mk_sc = at_scores(tt, h, first=(idx == 0))
            mk_ev = at_evmax(tt, h, mk_sc, first=(idx == 0))
            if prev is not None:
                mk_evp = at_trg_evp(prev[0], prev[1], mk_norm)
            mk_max = at_max(tt, h, mk_ev)
            mk_exp = at_exp(tt, h, mk_max)
            if prev is not None:
                at_pv(prev[0], prev[1], mk_evp)
            prev = (tt, h, mk_exp)
        mk_norm = at_norm(prev[0], prev[1], prev[2])
        mk_evp = at_trg_evp(prev[0], prev[1], mk_norm)
        at_pv(prev[0], prev[1], mk_evp)
        mark_O = Mark(c_s, c_s.n)
        if DBG:
            mo = dma("sync", c_dx, d_dbgO[h2], s_QO[:, :],
                     waits=mark_O.waits(), tag=f"dbgO{h2}")
            mark_O = Mark(c_dx, mo.val)

        for nqt in range(1):
            pass
        mk_gsc = Mark()
        for gb in range(1):
            triples = []
            for k in range(16):
                triples.append((G_h[:, k, :], XN_h[:, k, 0:512],
                                ps[7][0:8, 0:512], k == 0, False))
                triples.append((G_h[:, k, :], XN_l[:, k, 0:512],
                                ps[7][0:8, 0:512], False, False))
                triples.append((G_l[:, k, :], XN_h[:, k, 0:512],
                                ps[7][0:8, 0:512], False, k == 15))
            mm(7, triples, waits=list(mark_xn.waits()) + list(mk_Gp.waits()),
               tag=f"h{h2}gm")
            def evsg(eng):
                return eng.tensor_copy(SM_sgst, ps[7][0:8, 0:512])
            evac(7, "vector", evsg, tag=f"h{h2}ge")
            mk_sg = Mark(c_v, c_v.n)
            for jj in range(4):
                bank = 4 + jj % 2
                items = [(ps[bank][:, 0:8],
                          SM_sgst[0:8, 128 * jj:128 * (jj + 1)],
                          s_ident[0:8, 0:8])]
                trg(bank, items, waits=mk_sg.waits(), tag=f"h{h2}gT{jj}")
                def evgt(eng, jj=jj, bank=bank):
                    return eng.tensor_copy(SM_gsc[:, jj, :], ps[bank][:, 0:8])
                evac(bank, "vector", evgt, tag=f"h{h2}gTe{jj}")
            mk_gsc = Mark(c_v, c_v.n)
        for sj in range(4):
            sc = SM_gsc[:, sj, :]
            tmp = SM_gst[:, sj, :]
            st = SM_gstat[:, sj, :]
            P.emit("vector",
                   lambda eng, sc=sc, st=st: eng.reduce_max(st[:, 0:1], sc,
                                                            axis=AX, negate=True),
                   waits=mk_gsc.waits(), inc=c_v, tag=f"h{h2}g1_{sj}")
            mk1 = Mark(c_v, c_v.n)
            P.emit("scalar",
                   lambda eng, sc=sc, st=st: eng.activation(
                       sc, sc, ACT.Exp, bias=st[:, 0:1], accum_out=st[:, 1:2]),
                   waits=mk1.waits(), inc=c_s, tag=f"h{h2}g2_{sj}")
            mk2 = Mark(c_s, c_s.n)
            P.emit("vector",
                   lambda eng, st=st: eng.reciprocal(st[:, 2:3], st[:, 1:2]),
                   waits=mk2.waits(), inc=c_v, tag=f"h{h2}g3a_{sj}")
            mg1 = Mark(c_v, c_v.n)
            def fngb(eng, sc=sc, st=st):
                return eng.tensor_scalar_mul(sc, sc, st[:, 2:3])
            steps = [
                lambda eng, sc=sc, st=st: eng.reduce_max(st[:, 3:4], sc, axis=AX),
                lambda eng, sc=sc, tmp=tmp, st=st: eng.tensor_scalar(
                    tmp, sc, st[:, 3:4], -1e30, OP.is_equal, OP.mult),
                lambda eng, sc=sc, tmp=tmp: eng.tensor_add(tmp, sc, tmp),
                lambda eng, tmp=tmp, st=st: eng.reduce_max(st[:, 4:5], tmp, axis=AX),
                lambda eng, sc=sc, tmp=tmp, st=st: eng.tensor_scalar(
                    tmp, sc, st[:, 4:5], 0.0, OP.is_ge, OP.add),
                lambda eng, sc=sc, tmp=tmp: eng.tensor_mul(sc, sc, tmp),
            ]
            mprev = Mark(c_v, c_v.n)
            P.emit("vector", fngb, waits=mg1.waits(), inc=c_v, tag=f"h{h2}g3b_{sj}")
            mprev = Mark(c_v, c_v.n)
            for si, fstep in enumerate(steps):
                P.emit("vector", fstep, waits=mprev.waits(), inc=c_v,
                       tag=f"h{h2}g3_{sj}_{si}")
                mprev = Mark(c_v, c_v.n)
        mk_w = Mark(c_v, c_v.n)
        for sj in range(4):
            bank = 4 + sj % 2
            items = [(ps[bank][0:8, 0:128], SM_gsc[:, sj, :], s_ident[:, :])]
            trg(bank, items, waits=mk_w.waits(), tag=f"h{h2}wT{sj}")
            def evwf(eng, sj=sj, bank=bank):
                return eng.tensor_copy(SM_wfm[0:8, 128 * sj:128 * (sj + 1)],
                                       ps[bank][0:8, 0:128])
            evac(bank, "vector", evwf, tag=f"h{h2}wTe{sj}")
        mk_wfm = Mark(c_v, c_v.n)

        for fb in range(8):
            wbuf = fb % 2
            src = d_wo.rearrange("(k p) d -> p k d", p=128)[:, :, 256 * fb:256 * (fb + 1)]
            m_w = dma("sync", c_dw, BIG_wcb[wbuf][:, :, :], src,
                      waits=list(wcb_free[wbuf].waits()) +
                            (list(mark_O.waits()) if fb < 2 else []),
                      tag=f"h{h2}ow{fb}")
            for tt in range(4):
                bank = tt
                triples = [(QO_Q[:, k, 128 * tt:128 * (tt + 1)],
                            BIG_wcb[wbuf][:, k, :],
                            ps[bank][:, 0:256], k == 0, False) for k in range(16)]
                triples.append((s_ones[32:33, :], SM_browo[:, 256 * fb:256 * (fb + 1)],
                                ps[bank][:, 0:256], False, True))
                ws = list(m_w.waits())
                if fb == 0:
                    ws += list(mark_O.waits())
                if fb == 0 and tt == 0:
                    ws += list(acc_free.waits())
                mm(bank, triples, waits=ws, tag=f"h{h2}om{fb}_{tt}")
                wcb_free[wbuf] = Mark(c_t, c_t.n)
                def evw(eng, bank=bank, fb=fb, tt=tt):
                    return eng.copy(ACC_acc[:, tt, 256 * fb:256 * (fb + 1)],
                                    ps[bank][:, 0:256])
                evac(bank, "scalar", evw, tag=f"h{h2}oe{fb}_{tt}")
        mark_accW = Mark(c_s, c_s.n)


        mk_qo_ffn = Mark(c_t, c_t.n)
        lst_free = [mk_qo_ffn, mk_qo_ffn]
        for e in range(E):
            lbuf = e % 2
            m_l = dma("sync", c_dw, BIG_lstA[lbuf], d_lA[e],
                      waits=lst_free[lbuf].waits(), tag=f"h{h2}lA{e}")
            for i0 in range(0, 16, 4):
                bank = 4 + (i0 // 4) % 2
                items = [(ps[bank][0:128, 128 * jj:128 * jj + 16],
                          BIG_lstA[lbuf][:, 128 * (i0 + jj):128 * (i0 + jj + 1)],
                          s_ident[0:16, 0:16]) for jj in range(4)]
                trg(bank, items, waits=m_l.waits() if i0 == 0 else [],
                    tag=f"h{h2}lAT{e}_{i0}")
                def evlah(eng, e=e, i0=i0, bank=bank):
                    inst = None
                    for jj in range(4):
                        inst = eng.copy(
                            LAT_h[:, i0 + jj, 16 * e:16 * (e + 1)],
                            ps[bank][:, 128 * jj:128 * jj + 16])
                    return inst
                ws = mk_qo_ffn.waits() if (e == 0 and i0 == 0) else []
                P.emit("scalar", evlah, waits=list(bw[bank].waits()) + ws,
                       inc=c_s, tag=f"h{h2}lAh{e}_{i0}")
                mk_lah = Mark(c_s, c_s.n)
                def evlal(eng, e=e, i0=i0, bank=bank):
                    inst = None
                    for jj in range(4):
                        inst = eng.tensor_sub(
                            LAT_l[:, i0 + jj, 16 * e:16 * (e + 1)],
                            ps[bank][:, 128 * jj:128 * jj + 16],
                            LAT_h[:, i0 + jj, 16 * e:16 * (e + 1)])
                    return inst
                evac(bank, "vector", evlal, waits=mk_lah.waits(),
                     tag=f"h{h2}lAe{e}_{i0}")
            lst_free[lbuf] = Mark(c_t, c_t.n)
        mk_lAT = Mark(c_v, c_v.n)
        lstBn_free = [mk_qo_ffn, mk_qo_ffn]
        lstBr_free = mk_qo_ffn
        for e in range(E):
            lbuf = e % 2
            m_l = dma("sync", c_dw, BIG_stBn[lbuf][:, :, :],
                      d_lB[e].rearrange("(k p) r -> p k r", p=128),
                      waits=lstBn_free[lbuf].waits(), tag=f"h{h2}lB{e}")
            mk_ev = Mark()
            for i0 in range(0, 16, 4):
                bank = 4 + (i0 // 4) % 2
                items = [(ps[bank][0:16, 128 * jj:128 * (jj + 1)],
                          BIG_stBn[lbuf][:, i0 + jj, :],
                          s_ident[:, :]) for jj in range(4)]
                trg(bank, items, waits=m_l.waits() if i0 == 0 else [],
                    tag=f"h{h2}lBT{e}_{i0}")
                def evlbh(eng, i0=i0, bank=bank):
                    inst = None
                    for jj in range(4):
                        inst = eng.copy(
                            SBR_h[0:16, 128 * (i0 + jj):128 * (i0 + jj + 1)],
                            ps[bank][0:16, 128 * jj:128 * (jj + 1)])
                    return inst
                ws = lstBr_free.waits() if i0 == 0 else []
                P.emit("scalar", evlbh, waits=list(bw[bank].waits()) + ws,
                       inc=c_s, tag=f"h{h2}lBh{e}_{i0}")
                mk_lbh = Mark(c_s, c_s.n)
                def evlbl(eng, i0=i0, bank=bank):
                    inst = None
                    for jj in range(4):
                        inst = eng.tensor_sub(
                            SBR_l[0:16, 128 * (i0 + jj):128 * (i0 + jj + 1)],
                            ps[bank][0:16, 128 * jj:128 * (jj + 1)],
                            SBR_h[0:16, 128 * (i0 + jj):128 * (i0 + jj + 1)])
                    return inst
                evac(bank, "vector", evlbl, waits=mk_lbh.waits(),
                     tag=f"h{h2}lBe{e}_{i0}")
                mk_ev = Mark(c_v, c_v.n)
            lstBn_free[lbuf] = Mark(c_t, c_t.n)
            def mvlbh(eng, e=e):
                return eng.dma_start(
                    LBT_h[16 * e:16 * (e + 1), :, :].rearrange("p a b -> p (a b)"),
                    SBR_h)
            P.emit("gpsimd", mvlbh, waits=mk_ev.waits(), inc=c_dg, inc_amt=16,
                   tag=f"h{h2}lBmh{e}")
            def mvlbl(eng, e=e):
                return eng.dma_start(
                    LBT_l[16 * e:16 * (e + 1), :, :].rearrange("p a b -> p (a b)"),
                    SBR_l)
            P.emit("gpsimd", mvlbl, waits=mk_ev.waits(), inc=c_dg, inc_amt=16,
                   tag=f"h{h2}lBml{e}")
            lstBr_free = Mark(c_dg, c_dg.n)
        mk_lBT = lstBr_free

        for nq in range(1):
            bank = 0
            triples = []
            for k in range(16):
                triples.append((LAT_h[:, k, :], XN_h[:, k, 0:512],
                                ps[bank][:, 0:512], k == 0, False))
                triples.append((LAT_h[:, k, :], XN_l[:, k, 0:512],
                                ps[bank][:, 0:512], False, False))
                triples.append((LAT_l[:, k, :], XN_h[:, k, 0:512],
                                ps[bank][:, 0:512], False, k == 15))
            mm(bank, triples, waits=list(mk_lAT.waits()) + list(mark_xn.waits()),
               tag=f"h{h2}lrm")
            def evlr(eng, bank=bank):
                return eng.tensor_copy(SM_lr, ps[bank][:, 0:512])
            evac(bank, "vector", evlr, tag=f"h{h2}lre")
            mk_lrev = Mark(c_v, c_v.n)
            bank = 1
            triples = [(s_esel16[:, :], SM_wfm, ps[bank][:, 0:512], True, True)]
            mm(bank, triples, waits=mk_wfm.waits(), tag=f"h{h2}wrm")
            def evls(eng, bank=bank):
                return eng.tensor_mul(SM_lr, SM_lr, ps[bank][:, 0:512])
            evac(bank, "vector", evls, waits=mk_lrev.waits(), tag=f"h{h2}lrs")
        mk_lr = Mark(c_v, c_v.n)
        P.emit("scalar", lambda eng: eng.copy(LR_h, SM_lr),
               waits=mk_lr.waits(), inc=c_s, tag=f"h{h2}lrh")
        mk_lrh = Mark(c_s, c_s.n)
        P.emit("vector", lambda eng: eng.tensor_sub(LR_l, SM_lr, LR_h),
               waits=mk_lrh.waits(), inc=c_v, tag=f"h{h2}lrl")
        mk_lr = Mark(c_v, c_v.n)

        for fb in range(4):
            for tt in range(4):
                bank = (4 * fb + tt) % 4
                triples = [
                    (LR_h[:, 128 * tt:128 * (tt + 1)],
                     LBT_h[:, 4 * fb:4 * (fb + 1), :],
                     ps[bank][:, 0:512], True, False),
                    (LR_h[:, 128 * tt:128 * (tt + 1)],
                     LBT_l[:, 4 * fb:4 * (fb + 1), :],
                     ps[bank][:, 0:512], False, False),
                    (LR_l[:, 128 * tt:128 * (tt + 1)],
                     LBT_h[:, 4 * fb:4 * (fb + 1), :],
                     ps[bank][:, 0:512], False, True)]
                ws = (list(mk_lr.waits()) + list(mk_lBT.waits()) +
                      list(mark_accW.waits())) if (fb == 0 and tt == 0) else []
                mm(bank, triples, waits=ws, tag=f"h{h2}lm{fb}_{tt}")
                def evla2(eng, bank=bank, fb=fb, tt=tt):
                    sl = ACC_acc[:, tt, 512 * fb:512 * (fb + 1)]
                    return eng.tensor_add(sl, sl, ps[bank][:, 0:512])
                evac(bank, "vector", evla2, tag=f"h{h2}la{fb}_{tt}")

        wnat_free = Mark()
        dnat_free = Mark()
        wT_free = Mark()
        dT_free = Mark()
        gated_free = Mark()
        wT_first = [True]

        def dma_gate(e):
            return dma("gpsimd", c_dg, BIG_wnat2[:, :, :],
                       d_gate[e].rearrange("(k p) f -> p k f", p=128),
                       waits=list(wnat_free.waits()) +
                             (list(mark_O.waits()) + list(big_free.waits())
                              if e == 0 else []),
                       tag=f"h{h2}gw{e}")

        def split_gate(e, m_g):
            ws_e = list(mk_lBT.waits()) if wT_first[0] else []
            wT_first[0] = False
            P.emit("scalar",
                   lambda eng: eng.copy(WT_h[:, :, :], BIG_wnat2[:, :, :]),
                   waits=list(m_g.waits()) + list(wT_free.waits()) + ws_e,
                   inc=c_s, tag=f"h{h2}gTh{e}")
            mk_th = Mark(c_s, c_s.n)
            P.emit("vector",
                   lambda eng: eng.tensor_sub(WT_l[:, :, :], BIG_wnat2[:, :, :],
                                              WT_h[:, :, :]),
                   waits=mk_th.waits(), inc=c_v, tag=f"h{h2}gTl{e}")
            return Mark(c_v, c_v.n)

        m_gs = {0: dma_gate(0)}
        mk_gTs = {0: split_gate(0, m_gs[0])}
        for e in range(E):
            m_g = m_gs[e]
            mk_gT = mk_gTs[e]
            wnat_free = mk_gT
            m_d0 = dma("gpsimd", c_dg, BIG_dnat2[:, :],
                       d_down[e].rearrange("(c p) d -> c p d", p=128)[0],
                       waits=list(dnat_free.waits()) +
                             (list(mk_qo_ffn.waits()) if e == 0 else []),
                       tag=f"h{h2}dw{e}_0")
            ws_d = (list(mk_lBT.waits()) if e == 0 else []) + list(dT_free.waits())
            P.emit("scalar",
                   lambda eng: eng.copy(DT_h[:, 0, :], BIG_dnat2[:, :]),
                   waits=list(m_d0.waits()) + ws_d,
                   inc=c_s, tag=f"h{h2}dTh{e}_0")
            mk_dh = Mark(c_s, c_s.n)
            P.emit("vector",
                   lambda eng: eng.tensor_sub(
                       DT_l[:, 0, :], BIG_dnat2[:, :], DT_h[:, 0, :]),
                   waits=mk_dh.waits(), inc=c_v, tag=f"h{h2}dTl{e}_0")
            dnat_free = Mark(c_v, c_v.n)
            m_u = dma("gpsimd", c_dg, BIG_wnat2[:, :, :],
                      d_up[e].rearrange("(k p) f -> p k f", p=128),
                      waits=mk_gT.waits(), tag=f"h{h2}uw{e}")
            m_d1 = dma("gpsimd", c_dg, BIG_dnat2[:, :],
                       d_down[e].rearrange("(c p) d -> c p d", p=128)[1],
                       waits=dnat_free.waits(), tag=f"h{h2}dw{e}_1")
            def mm3w(bank, fc, waits, tag2):
                triples = []
                for k in range(16):
                    wh = WT_h[:, k, 128 * fc:128 * (fc + 1)]
                    wl = WT_l[:, k, 128 * fc:128 * (fc + 1)]
                    triples.append((wh, XN_h[:, k, 0:512], ps[bank][:, 0:512],
                                    k == 0, False))
                    triples.append((wh, XN_l[:, k, 0:512], ps[bank][:, 0:512],
                                    False, False))
                    triples.append((wl, XN_h[:, k, 0:512], ps[bank][:, 0:512],
                                    False, k == 15))
                mm(bank, triples, waits=waits, tag=tag2)

            mk_gmm = []
            mk_uhalf = []
            for fc in range(2):
                bank = fc
                mm3w(bank, fc, mk_gT.waits() if fc == 0 else [],
                     f"h{h2}gm{e}_{fc}")
                mk_gmm.append(Mark(c_t, c_t.n))
                def evgs(eng, bank=bank, fc=fc):
                    return eng.activation(QO_gt[:, fc, :], ps[bank][:, 0:512],
                                          ACT.Silu)
                evac(bank, "scalar", evgs,
                     waits=gated_free.waits() if fc == 0 else [],
                     tag=f"h{h2}gs{e}_{fc}")
                P.emit("scalar",
                       lambda eng, fc=fc: eng.copy(
                           WT_h[:, :, 128 * fc:128 * (fc + 1)],
                           BIG_wnat2[:, :, 128 * fc:128 * (fc + 1)]),
                       waits=list(m_u.waits()) + list(mk_gmm[fc].waits()),
                       inc=c_s, tag=f"h{h2}uTh{e}_{fc}")
                mk_uh = Mark(c_s, c_s.n)
                P.emit("vector",
                       lambda eng, fc=fc: eng.tensor_sub(
                           WT_l[:, :, 128 * fc:128 * (fc + 1)],
                           BIG_wnat2[:, :, 128 * fc:128 * (fc + 1)],
                           WT_h[:, :, 128 * fc:128 * (fc + 1)]),
                       waits=mk_uh.waits(), inc=c_v, tag=f"h{h2}uTl{e}_{fc}")
                mk_uhalf.append(Mark(c_v, c_v.n))
            mk_gsilu = Mark(c_s, c_s.n)
            wnat_free = mk_uhalf[1]
            P.emit("scalar",
                   lambda eng: eng.copy(DT_h[:, 1, :], BIG_dnat2[:, :]),
                   waits=list(m_d1.waits()), inc=c_s, tag=f"h{h2}dTh{e}_1")
            mk_dh = Mark(c_s, c_s.n)
            P.emit("vector",
                   lambda eng: eng.tensor_sub(
                       DT_l[:, 1, :], BIG_dnat2[:, :], DT_h[:, 1, :]),
                   waits=mk_dh.waits(), inc=c_v, tag=f"h{h2}dTl{e}_1")
            dnat_free = Mark(c_v, c_v.n)
            mk_dT = Mark(c_v, c_v.n)
            if e + 1 < E:
                m_gs[e + 1] = dma_gate(e + 1)
            for fc in range(2):
                bank = fc
                mm3w(bank, fc, mk_uhalf[fc].waits(),
                     f"h{h2}um{e}_{fc}")
                def evu(eng, bank=bank, fc=fc):
                    return eng.tensor_copy(QO_guf[:, fc, :], ps[bank][:, 0:512])
                evac(bank, "vector", evu, tag=f"h{h2}ue{e}_{fc}")
                mk_u_ev = Mark(c_v, c_v.n)
                def fgt(eng, fc=fc):
                    return eng.tensor_mul(QO_gt[:, fc, :], QO_gt[:, fc, :],
                                          QO_guf[:, fc, :])
                P.emit("vector", fgt,
                       waits=[(c_s, mk_gsilu.val), (c_v, mk_u_ev.val)], inc=c_v,
                       tag=f"h{h2}gt{e}_{fc}")
                mk_t = Mark(c_v, c_v.n)
                P.emit("scalar",
                       lambda eng, fc=fc: eng.copy(QO_gh[:, fc, :],
                                                   QO_gt[:, fc, :]),
                       waits=mk_t.waits(), inc=c_s, tag=f"h{h2}gth{e}_{fc}")
                mk_th2 = Mark(c_s, c_s.n)
                P.emit("vector",
                       lambda eng, fc=fc: eng.tensor_sub(QO_gl[:, fc, :],
                                                         QO_gt[:, fc, :],
                                                         QO_gh[:, fc, :]),
                       waits=mk_th2.waits(), inc=c_v, tag=f"h{h2}gtl{e}_{fc}")
            mk_gated = Mark(c_v, c_v.n)
            wT_free = Mark(c_t, c_t.n)
            if e + 1 < E:
                mk_gTs[e + 1] = split_gate(e + 1, m_gs[e + 1])

            for fb in range(4):
                for tt in range(4):
                    bank = (4 * fb + tt) % 4
                    triples = []
                    for fc in range(2):
                        gh = QO_gh[:, fc, 128 * tt:128 * (tt + 1)]
                        gl = QO_gl[:, fc, 128 * tt:128 * (tt + 1)]
                        dh = DT_h[:, fc, 512 * fb:512 * (fb + 1)]
                        dl = DT_l[:, fc, 512 * fb:512 * (fb + 1)]
                        triples.append((gh, dh, ps[bank][:, 0:512],
                                        fc == 0, False))
                        triples.append((gh, dl, ps[bank][:, 0:512],
                                        False, False))
                        triples.append((gl, dh, ps[bank][:, 0:512],
                                        False, fc == 1))
                    ws = []
                    if fb == 0 and tt == 0:
                        ws = list(mk_dT.waits()) + list(mk_gated.waits())
                        if e == 0:
                            ws += list(mark_accW.waits())
                    mm(bank, triples, waits=ws, tag=f"h{h2}dm{e}_{fb}_{tt}")
                    def evda(eng, bank=bank, fb=fb, tt=tt, e=e):
                        sl = ACC_acc[:, tt, 512 * fb:512 * (fb + 1)]
                        return eng.scalar_tensor_tensor(
                            sl, ps[bank][:, 0:512], SM_gsc[:, tt, e:e + 1], sl,
                            OP.mult, OP.add)
                    evac(bank, "vector", evda, tag=f"h{h2}da{e}_{fb}_{tt}")
            gated_free = Mark(c_t, c_t.n)
            dT_free = Mark(c_t, c_t.n)
        mark_acc = Mark(c_v, c_v.n)
        qo_free = Mark(c_t, c_t.n)
        big_free_ffn = Mark(c_t, c_t.n)

        xf_gate = [dnat_free, wT_free, wnat_free, wnat_free]
        m_xs = []
        for tt in range(4):
            m_xs.append(dma("sync", c_dx, BIG_xf4[tt],
                            d_xown[:, :].rearrange("(q p) d -> q p d", p=128)[4 * h2 + tt],
                            waits=xf_gate[tt].waits(), tag=f"h{h2}fx{tt}"))
        st_m = []
        for tt in range(4):
            def fadd(eng, tt=tt):
                return eng.tensor_add(BIG_xf4[tt], ACC_acc[:, tt, :],
                                      BIG_xf4[tt])
            P.emit("vector", fadd,
                   waits=list(m_xs[tt].waits()) + (list(mark_acc.waits()) if tt == 0 else []),
                   inc=c_v, tag=f"h{h2}fa{tt}")
            mk_ob = Mark(c_v, c_v.n)
            def fst(eng, tt=tt, h2=h2):
                return eng.dma_start(
                    d_out[:, :].rearrange("(q p) d -> q p d", p=128)[4 * h2 + tt],
                    BIG_xf4[tt])
            P.emit("sync", fst, waits=mk_ob.waits(), inc=c_st, inc_amt=16,
                   tag=f"h{h2}fs{tt}")
            st_m.append(Mark(c_st, c_st.n))
        acc_free = Mark(c_v, c_v.n)
        big_free = Mark(c_st, c_st.n)
        big_free_b = st_m[1]
        xn_free = qo_free
        xn_free = Mark(c_t, c_t.n)

    P.emit("sync", lambda eng: None, waits=[(c_st, c_st.n)], tag="final")

    P.audit()
    with nc.Block() as block:
        P.lower(block)
    stack.close()
    return nc


_NC_CACHE = None


def _get_nc():
    global _NC_CACHE
    if _NC_CACHE is None:
        _NC_CACHE = build_nc()
    return _NC_CACHE


def kernel(L0_attn_x, L0_attn_Wq, L0_attn_bq, L0_attn_Wk, L0_attn_bk,
           L0_attn_Wv, L0_attn_bv, L0_attn_Wo, L0_attn_bo, L0_ffn_G,
           L0_ffn_gate_w, L0_ffn_up_w, L0_ffn_down_w, L0_ffn_lora_A,
           L0_ffn_lora_B):
    x = np.ascontiguousarray(np.asarray(L0_attn_x, dtype=np.float32))
    ident = np.eye(128, dtype=np.float32)
    esel16 = np.zeros((8, 128), dtype=np.float32)
    for e in range(8):
        esel16[e, 16 * e:16 * (e + 1)] = 1.0
    ones1 = np.ones((33, 128), dtype=np.float32)
    bvrep = np.ascontiguousarray(
        np.broadcast_to(np.asarray(L0_attn_bv, np.float32), (128, 256)))
    bqbo = np.stack([np.asarray(L0_attn_bq, np.float32),
                     np.asarray(L0_attn_bo, np.float32)])
    com = {
        "wq": np.ascontiguousarray(L0_attn_Wq, np.float32),
        "wk": np.ascontiguousarray(L0_attn_Wk, np.float32),
        "wv": np.ascontiguousarray(L0_attn_Wv, np.float32),
        "wo": np.ascontiguousarray(L0_attn_Wo, np.float32),
        "bqbo": np.ascontiguousarray(bqbo),
        "bk": np.ascontiguousarray(np.asarray(L0_attn_bk, np.float32).reshape(2, 128).T),
        "bvrep": bvrep,
        "gG": np.ascontiguousarray(L0_ffn_G, np.float32),
        "gate_w": np.ascontiguousarray(
            np.transpose(np.asarray(L0_ffn_gate_w, np.float32), (0, 2, 1))),
        "up_w": np.ascontiguousarray(
            np.transpose(np.asarray(L0_ffn_up_w, np.float32), (0, 2, 1))),
        "down_w": np.ascontiguousarray(
            np.transpose(np.asarray(L0_ffn_down_w, np.float32), (0, 2, 1))),
        "lora_A": np.ascontiguousarray(L0_ffn_lora_A, np.float32),
        "lora_B": np.ascontiguousarray(L0_ffn_lora_B, np.float32),
        "ident": ident,
        "epsv": np.full((128, 1), EPS_HOST, np.float32),
        "esel16": esel16,
        "ones1": ones1,
    }
    in_maps = []
    for core in range(8):
        b, p = divmod(core, 2)
        xb = x[b]
        xown = np.ascontiguousarray(xb.reshape(16, 128, D)[p::2].reshape(OWN, D))
        r = 128 * p
        jj = np.arange(256)[None, :]
        pp = np.arange(128)[:, None]
        dmask = np.where(jj <= pp + r, 0.0, -1e30).astype(np.float32)
        m = dict(com)
        m["xseq"] = xb
        m["xown"] = xown
        m["dmask"] = np.ascontiguousarray(dmask)
        in_maps.append(m)

    nc = _get_nc()
    res = run_bass_kernel_spmd(nc, in_maps, core_ids=list(range(8)))
    out = np.empty((B, T, D), dtype=np.float32)
    for core in range(8):
        b, p = divmod(core, 2)
        o = np.asarray(res.results[core]["out"]).reshape(8, 128, D)
        out.reshape(B, 16, 128, D)[b, p::2] = o
    return out
